# version 10
# baseline (speedup 1.0000x reference)
import math
from contextlib import ExitStack

import numpy as np
import ml_dtypes

import concourse.bass as bass
import concourse.mybir as mybir
from concourse.bass_utils import run_bass_kernel_spmd

F32 = mybir.dt.float32
BF16 = mybir.dt.bfloat16
U8 = mybir.dt.uint8
AF = mybir.ActivationFunctionType
ALU = mybir.AluOpType
AX = mybir.AxisListType

T = 4096
D = 2048
ND = D // 128
HD = 128
NQH = 16
EPS = np.float32(1e-5)
ISQ = np.float32(1.0 / math.sqrt(HD))
QB = 512
NQB = T // QB
I0 = 5632
I1 = 8192

ENGINES = ("sync", "gpsimd", "scalar", "vector", "tensor")


class Sem:
    __slots__ = ("h", "n")

    def __init__(self, h):
        self.h = h
        self.n = 0


class Ph:
    def __init__(self, nc, stack, tag, pstack=None):
        self.nc = nc
        self.stack = stack
        self.pstack = pstack if pstack is not None else stack
        self.tag = tag
        self.ops = {e: [] for e in ENGINES}
        self.sems = []

    def sem(self, name):
        s = Sem(self.stack.enter_context(self.nc.semaphore(f"{self.tag}_{name}")))
        self.sems.append(s)
        return s

    def dsem(self, name):
        s = Sem(self.pstack.enter_context(
            self.nc.semaphore(f"{self.tag}_{name}")))
        self.sems.append(s)
        return s

    def do(self, eng, fn, waits=(), inc=None, n=1):
        w = [(s.h, t) for (s, t) in waits if t > 0]
        i = None
        if inc is not None:
            i = (inc.h, n)
            inc.n += n
        self.ops[eng].append((fn, w, i))

    def run(self):
        nc = self.nc

        def mk(lst):
            def body(eng):
                for fn, waits, inc in lst:
                    for sh, tgt in waits:
                        eng.wait_ge(sh, tgt)
                    inst = fn(eng)
                    if inc is not None:
                        inst.then_inc(inc[0], inc[1])
            return body

        sems = list(self.sems)
        if sems:
            with nc.Block(self.tag + "_clr") as blk:
                @blk.gpsimd
                def _(g):
                    for s in sems:
                        g.sem_clear(s.h)
        with nc.Block(self.tag) as blk:
            for e in ENGINES:
                if self.ops[e]:
                    getattr(blk, e)(mk(self.ops[e]))


def _split_dma(ph, queue, out_v, in_v, ndim0, per, sem, waits0=()):
    i = 0
    first = True
    while i < ndim0:
        j = min(ndim0, i + per)
        def fn(e, i=i, j=j):
            return e.dma_start(out=out_v[:, i:j], in_=in_v[:, i:j])
        ph.do(queue, fn, waits=waits0 if first else (), inc=sem, n=16)
        first = False
        i = j


def emit_ln(ph, xin_ap, out_ap, mv24, mvc, in_waits, s_mv, s_sd, s_ln, s_out,
            out_waits=()):
    def fn(e, xin_ap=xin_ap):
        r = None
        for k in range(4):
            r = e.bn_stats(mv24[:, k * 6:(k + 1) * 6], xin_ap[:, k * 512:(k + 1) * 512])
        return r
    ph.do("vector", fn, waits=in_waits, inc=s_mv)
    for k in range(4):
        def fn(e, k=k):
            return e.bn_aggr(mvc[:, k:k + 5:4], mv24[:, k * 6:(k + 1) * 6])
        ph.do("vector", fn, waits=[(s_mv, s_mv.n)] if k == 0 else [], inc=s_mv)

    def fn(e):
        e.reduce_sum(mvc[:, 8:9], mvc[:, 0:4], axis=AX.X)
        e.reduce_sum(mvc[:, 9:10], mvc[:, 4:8], axis=AX.X)
        return e.tensor_mul(mv24[:, 0:4], mvc[:, 0:4], mvc[:, 0:4])
    ph.do("vector", fn, waits=[(s_mv, s_mv.n)], inc=s_mv)

    def fn(e):
        e.reduce_sum(mvc[:, 10:11], mv24[:, 0:4], axis=AX.X)
        return e.tensor_scalar(mvc[:, 12:13], mvc[:, 8:9], 0.25, None, ALU.mult)
    ph.do("vector", fn, waits=[(s_mv, s_mv.n)], inc=s_mv)

    def fn(e):
        e.tensor_add(mvc[:, 11:12], mvc[:, 9:10], mvc[:, 10:11])
        return e.tensor_mul(mvc[:, 13:14], mvc[:, 12:13], mvc[:, 12:13])
    ph.do("vector", fn, waits=[(s_mv, s_mv.n)], inc=s_mv)

    def fn(e):
        return e.tensor_scalar(mvc[:, 14:15], mvc[:, 11:12], 0.25, None, ALU.mult)
    ph.do("vector", fn, waits=[(s_mv, s_mv.n)], inc=s_mv)

    def fn(e):
        return e.tensor_sub(mvc[:, 14:15], mvc[:, 14:15], mvc[:, 13:14])
    ph.do("vector", fn, waits=[(s_mv, s_mv.n)], inc=s_mv)

    def fn(e):
        return e.activation(mvc[:, 14:15], mvc[:, 14:15], AF.Sqrt, bias=EPS)
    ph.do("scalar", fn, waits=[(s_mv, s_mv.n)], inc=s_sd)

    def fn(e):
        return e.reciprocal(mvc[:, 15:16], mvc[:, 14:15])
    ph.do("vector", fn, waits=[(s_sd, s_sd.n)], inc=s_ln)

    def fn(e):
        return e.tensor_scalar(mvc[:, 16:17], mvc[:, 12:13], mvc[:, 15:16], -1.0,
                               ALU.mult, ALU.mult)
    ph.do("vector", fn, waits=[(s_ln, s_ln.n)], inc=s_ln)

    def fn(e, xin_ap=xin_ap, out_ap=out_ap):
        return e.activation(out_ap, xin_ap, AF.Identity,
                            bias=mvc[:, 16:17], scale=mvc[:, 15:16])
    ph.do("scalar", fn, waits=[(s_ln, s_ln.n)] + list(out_waits), inc=s_out)


def phase_qkv(nc, stack, G, P):
    TTQ = P["TTQ"]
    NTT = T // TTQ
    NC = TTQ // 128
    rope = P["rope"]
    nkv = P["nkv"]
    NXB, NHB, NQR, NSP = 3, 2, 2, 4

    with ExitStack() as bs:
        ph = Ph(nc, bs, P["tag"], pstack=stack)
        def sb(name, shape, dtype):
            return bs.enter_context(nc.sbuf_tensor(f"{P['tag']}_{name}", shape, dtype))
        xin = sb("xin", [128, NXB * 2048], F32)
        h_nat = sb("hnat", [128, NHB * 2048], BF16)
        hT = sb("hT", [128, ND * TTQ], BF16)
        wres = sb("wres", [128, ND * 2048], BF16)
        qraw = sb("qraw", [128, NQR * 512], BF16)
        tmpsw = sb("tmpsw", [128, NQR * 512], BF16)
        spq = sb("spq", [128, NSP * TTQ], BF16)
        sqb = sb("sqb", [128, 512], BF16)
        mv24 = sb("mv24", [128, 24], F32)
        mvc = sb("mvc", [128, 18], F32)
        norms = sb("norms", [1, 64], F32)
        biases = sb("biases", [128, 32], F32)
        brow = sb("brow", [1, 16], F32)
        browb = sb("browb", [1, 16], BF16)
        bbias = sb("bbias", [128, 16], F32)
        psT = bs.enter_context(nc.psum_tensor(f"{P['tag']}_psT", [128, 2048], F32))
        psP = bs.enter_context(nc.psum_tensor(f"{P['tag']}_psP", [128, 2048], F32))

        s_w = ph.dsem("w")
        s_x = [ph.dsem(f"x{i}") for i in range(NXB)]
        x_slot_cnt = [0] * NXB
        s_spl = [ph.dsem(f"spl{i}") for i in range(NSP)]
        sp_slot_cnt = [0] * NSP
        s_mv = ph.sem("mv")
        s_sd = ph.sem("sd")
        s_ln = ph.sem("ln")
        s_h = ph.sem("h")
        s_tr = ph.sem("tr")
        s_te = ph.sem("te")
        s_pq = ph.sem("pq")
        s_eb = ph.sem("eb")
        s_rp = ph.sem("rp")
        s_ev = ph.sem("ev")
        s_evs = ph.sem("evs")
        s_sq = ph.sem("sq")
        s_nm = ph.sem("nm")
        s_nr = ph.sem("nr")
        s_sp = ph.dsem("sp")
        s_bb = ph.sem("bb")

        hT3 = hT[:, :].rearrange("p (d t) -> p d t", d=ND)
        wres3 = wres[:, :].rearrange("p (d c) -> p d c", d=ND)
        ident = G["ident"]
        ones_row = G["ones_row"]
        ones_col = G["ones_col"]
        cosb = G["cos"]
        sinb = G["sin"]
        R = P["R_in"]

        s_cl = ph.dsem("cl")
        if rope:
            def fn(e):
                return e.dma_start(out=biases[:, 0:16], in_=P["bq"])
            ph.do("sync", fn, inc=s_cl, n=16)
            def fn(e):
                return e.dma_start(out=biases[:, 16:16 + nkv], in_=P["bk"])
            ph.do("sync", fn, inc=s_cl, n=16)
        bias_t = s_cl.n

        def fn(e):
            return e.memset(norms[0:1, :], 0.0)
        ph.do("vector", fn, inc=s_bb)

        cnt = dict(x=0, h=0, te=0, pq=0, ev=0, sq=0, nm=0, nr=0, sp=0, rp=0)
        ev_tokens = []
        rope_cnt = [0]
        slot_last_sp = [None] * NSP
        slot_last_sq = [None] * NSP
        pq_end_prev_tile = 0

        for wp, group in enumerate(P["wpasses"]):
            col0 = 0
            wguard = [(s_pq, cnt["pq"])] if cnt["pq"] > 0 else []
            for kind, noc, wap, ocbase in group:
                src = wap.rearrange("(d p) o -> p d o", p=128)[
                    :, :, ocbase * 128:(ocbase + noc) * 128]
                dstv = wres3[:, :, col0:col0 + noc * 128]
                _split_dma(ph, "gpsimd", dstv, src, ND, 8, s_w, waits0=wguard)
                wguard = []
                col0 += noc * 128
            w_t = s_w.n

            for tt in range(NTT):
                for c in range(NC):
                    ci = cnt["x"]
                    xs = ci % NXB
                    r0 = tt * TTQ + c * 128
                    waits = []
                    if ci >= NXB:
                        waits.append((s_h, ci - NXB + 1))
                    def fn(e, xs=xs, r0=r0):
                        return e.dma_start(out=xin[:, xs * 2048:(xs + 1) * 2048],
                                           in_=R[r0:r0 + 128, :])
                    ph.do("sync", fn, waits=waits, inc=s_x[xs], n=16)
                    cnt["x"] += 1
                    x_slot_cnt[xs] += 1
                    hs = ci % NHB
                    ow = []
                    if ci >= NHB:
                        ow.append((s_tr, 2 * (ci - NHB) + 2))
                    emit_ln(ph, xin[:, xs * 2048:(xs + 1) * 2048],
                            h_nat[:, hs * 2048:(hs + 1) * 2048],
                            mv24, mvc, [(s_x[xs], 16 * x_slot_cnt[xs])],
                            s_mv, s_sd, s_ln, s_h, out_waits=ow)
                    cnt["h"] += 1
                    for half in range(2):
                        tseq = 2 * ci + half
                        pst = psT[:, (tseq % 2) * 1024:((tseq % 2) + 1) * 1024]
                        waits = [(s_h, cnt["h"])]
                        if tseq >= 2:
                            waits.append((s_te, tseq - 1))
                        def fn(e, hs=hs, half=half, pst=pst):
                            r = None
                            for dd in range(8):
                                d = half * 8 + dd
                                r = e.matmul(pst[:, dd * 128:(dd + 1) * 128],
                                             h_nat[:, hs * 2048 + d * 128:hs * 2048 + (d + 1) * 128],
                                             ident[:, :], start=True, stop=True)
                            return r
                        ph.do("tensor", fn, waits=waits, inc=s_tr)
                        waits = [(s_tr, s_tr.n)]
                        if pq_end_prev_tile and cnt["te"] % (2 * NC) < 2:
                            waits.append((s_pq, pq_end_prev_tile))
                        def fn(e, pst=pst, half=half, c=c):
                            return e.copy(
                                hT3[:, half * 8:(half + 1) * 8, c * 128:(c + 1) * 128],
                                pst.rearrange("p (d t) -> p d t", d=8))
                        ph.do("scalar", fn, waits=waits, inc=s_te)
                        cnt["te"] += 1
                te_tile = cnt["te"]
                te_base = te_tile - 2 * NC

                colbase = 0
                for kind, noc, wap, ocbase in group:
                    for oc in range(noc):
                        spslot = cnt["sp"] % NSP
                        first_nt = True
                        for nt in range(TTQ // 512):
                            bank = cnt["pq"] % 2
                            pst = psP[:, bank * 512:(bank + 1) * 512]
                            waits = [(s_te, te_base + 8 * (nt + 1)), (s_w, w_t)]
                            if cnt["pq"] >= 2:
                                waits.append(ev_tokens[-2])
                            def fn(e, nt=nt, oc=oc, colbase=colbase, pst=pst):
                                r = None
                                for d in range(ND):
                                    r = e.matmul(pst[:, 0:512],
                                                 wres3[:, d, (colbase + oc) * 128:(colbase + oc + 1) * 128],
                                                 hT3[:, d, nt * 512:(nt + 1) * 512],
                                                 start=(d == 0), stop=(d == ND - 1))
                                return r
                            ph.do("tensor", fn, waits=waits, inc=s_pq)
                            cnt["pq"] += 1

                            if first_nt:
                                spill_guard = ([] if slot_last_sp[spslot] is None else
                                               [(s_spl[slot_last_sp[spslot][0]],
                                                 slot_last_sp[spslot][1])])
                                if slot_last_sq[spslot] is not None:
                                    spill_guard = spill_guard + [(s_sq, slot_last_sq[spslot])]
                            else:
                                spill_guard = []
                            first_nt = False
                            if rope and kind in ("q", "k"):
                                qs = cnt["ev"] % NQR
                                bcol = oc if kind == "q" else 16 + oc
                                waits = [(s_pq, cnt["pq"]), (s_cl, bias_t)] + spill_guard
                                if rope_cnt[0] >= NQR:
                                    waits.append((s_ev, rope_cnt[0] - NQR + 1))
                                def fn(e, qs=qs, bcol=bcol, pst=pst):
                                    return e.activation(qraw[:, qs * 512:(qs + 1) * 512],
                                                        pst[:, 0:512], AF.Identity,
                                                        bias=biases[:, bcol:bcol + 1], scale=np.float32(1.0))
                                ph.do("scalar", fn, waits=waits, inc=s_eb)
                                def fn(e, qs=qs, bcol=bcol, pst=pst):
                                    return e.activation(tmpsw[0:64, qs * 512:(qs + 1) * 512],
                                                        pst[64:128, 0:512], AF.Identity,
                                                        bias=biases[64:128, bcol:bcol + 1],
                                                        scale=np.float32(1.0))
                                ph.do("scalar", fn, inc=s_eb)
                                def fn(e, qs=qs, bcol=bcol, pst=pst):
                                    return e.activation(tmpsw[64:128, qs * 512:(qs + 1) * 512],
                                                        pst[0:64, 0:512], AF.Identity,
                                                        bias=biases[0:64, bcol:bcol + 1],
                                                        scale=np.float32(1.0))
                                ph.do("scalar", fn, inc=s_eb)
                                sw_t = s_eb.n
                                tc0 = tt * TTQ + nt * 512
                                def fn(e, qs=qs, tc0=tc0):
                                    return e.tensor_mul(qraw[:, qs * 512:(qs + 1) * 512],
                                                        qraw[:, qs * 512:(qs + 1) * 512],
                                                        cosb[:, tc0:tc0 + 512])
                                ph.do("vector", fn, waits=[(s_eb, sw_t)], inc=s_rp)
                                cnt["rp"] += 1
                                def fn(e, qs=qs, tc0=tc0):
                                    return e.tensor_mul(tmpsw[:, qs * 512:(qs + 1) * 512],
                                                        tmpsw[:, qs * 512:(qs + 1) * 512],
                                                        sinb[:, tc0:tc0 + 512])
                                ph.do("vector", fn, inc=s_rp)
                                cnt["rp"] += 1
                                def fn(e, qs=qs, spslot=spslot, nt=nt):
                                    return e.tensor_add(
                                        spq[:, spslot * TTQ + nt * 512:spslot * TTQ + (nt + 1) * 512],
                                        qraw[:, qs * 512:(qs + 1) * 512],
                                        tmpsw[:, qs * 512:(qs + 1) * 512])
                                ph.do("vector", fn, waits=[(s_rp, cnt["rp"])], inc=s_ev)
                                cnt["ev"] += 1
                                rope_cnt[0] += 1
                                ev_tokens.append((s_ev, s_ev.n))
                            else:
                                def fn(e, pst=pst, spslot=spslot, nt=nt):
                                    return e.copy(
                                        spq[:, spslot * TTQ + nt * 512:spslot * TTQ + (nt + 1) * 512],
                                        pst[:, 0:512])
                                ph.do("scalar", fn,
                                      waits=[(s_pq, cnt["pq"])] + spill_guard, inc=s_evs)
                                cnt["ev"] += 1
                                ev_tokens.append((s_evs, s_evs.n))

                            if kind in ("q", "k"):
                                ncx = oc if kind == "q" else 16 + oc
                                def fn(e, spslot=spslot, nt=nt):
                                    return e.tensor_mul(
                                        sqb[:, :],
                                        spq[:, spslot * TTQ + nt * 512:spslot * TTQ + (nt + 1) * 512],
                                        spq[:, spslot * TTQ + nt * 512:spslot * TTQ + (nt + 1) * 512])
                                ph.do("vector", fn, waits=[ev_tokens[-1]], inc=s_sq)
                                cnt["sq"] += 1
                                def fn(e):
                                    return e.matmul(psP[0:1, 1024:1536], ones_col[:, :],
                                                    sqb[:, :], start=True, stop=True)
                                ph.do("tensor", fn,
                                      waits=[(s_sq, cnt["sq"]), (s_nr, cnt["nr"])],
                                      inc=s_nm)
                                cnt["nm"] += 1
                                def fn(e):
                                    return e.reduce_max(norms[0:1, 63:64],
                                                        psP[0:1, 1024:1536], axis=AX.X)
                                ph.do("vector", fn, waits=[(s_nm, cnt["nm"])], inc=s_nr)
                                cnt["nr"] += 1
                                def fn(e, ncx=ncx):
                                    return e.tensor_max(norms[0:1, ncx:ncx + 1],
                                                        norms[0:1, ncx:ncx + 1],
                                                        norms[0:1, 63:64])
                                ph.do("vector", fn, waits=[(s_nr, cnt["nr"])], inc=s_nr)
                                cnt["nr"] += 1

                        dst_ap, row0 = P["dst"][kind]
                        col0 = tt * TTQ
                        def fn(e, dst_ap=dst_ap, row0=row0, oc=oc, spslot=spslot, col0=col0):
                            return e.dma_start(
                                out=dst_ap[(row0 + oc) * 128:(row0 + oc + 1) * 128,
                                           col0:col0 + TTQ],
                                in_=spq[:, spslot * TTQ:(spslot + 1) * TTQ])
                        ph.do("gpsimd", fn, waits=[ev_tokens[-1]],
                              inc=s_spl[spslot], n=16)
                        cnt["sp"] += 1
                        sp_slot_cnt[spslot] += 1
                        slot_last_sp[spslot] = (spslot, 16 * sp_slot_cnt[spslot])
                        slot_last_sq[spslot] = cnt["sq"] if kind in ("q", "k") else None
                    colbase += noc
                pq_end_prev_tile = cnt["pq"]

        for h in range(16):
            kcol = 16 + h // (16 // nkv)
            def fn(e, h=h, kcol=kcol):
                return e.tensor_tensor(norms[0:1, 32 + h:33 + h], norms[0:1, h:h + 1],
                                       norms[0:1, kcol:kcol + 1], ALU.mult)
            ph.do("vector", fn, waits=[(s_nr, cnt["nr"])] if h == 0 else [], inc=s_bb)
        def fn(e):
            return e.activation(norms[0:1, 32:48], norms[0:1, 32:48], AF.Sqrt)
        ph.do("scalar", fn, waits=[(s_bb, s_bb.n)], inc=s_bb)
        def fn(e):
            return e.mul(brow[0:1, 0:16], norms[0:1, 32:48], np.float32(-ISQ))
        ph.do("scalar", fn, waits=[(s_bb, s_bb.n)], inc=s_bb)
        def fn(e):
            return e.tensor_copy(browb[0:1, 0:16], brow[0:1, 0:16])
        ph.do("vector", fn, waits=[(s_bb, s_bb.n)], inc=s_bb)
        def fn(e):
            return e.matmul(psP[:, 1536:1552], ones_row[0:1, :], browb[0:1, 0:16],
                            start=True, stop=True)
        ph.do("tensor", fn, waits=[(s_bb, s_bb.n), (s_nm, cnt["nm"])], inc=s_bb)
        def fn(e):
            return e.tensor_copy(bbias[:, 0:16], psP[:, 1536:1552])
        ph.do("vector", fn, waits=[(s_bb, s_bb.n)], inc=s_bb)
        def fn(e):
            return e.dma_start(out=P["bbias_d"], in_=bbias[:, 0:16])
        ph.do("gpsimd", fn, waits=[(s_bb, s_bb.n)], inc=s_sp, n=16)
        ph.do("gpsimd", lambda e: e.nop(),
              waits=[(s_sp, 16)] + [(s_spl[i], 16 * sp_slot_cnt[i]) for i in range(NSP)])
        ph.do("sync", lambda e: e.nop(),
              waits=[(s_x[i], 16 * x_slot_cnt[i]) for i in range(NXB)])
        ph.run()


def phase_attn(nc, stack, G, P):
    nkv = P["nkv"]
    hpg = NQH // nkv
    has_bias = P["rope"]

    with ExitStack() as bs:
        ph = Ph(nc, bs, P["tag"], pstack=stack)
        def sb(name, shape, dtype):
            return bs.enter_context(nc.sbuf_tensor(f"{P['tag']}_{name}", shape, dtype))
        wo = sb("wo", [128, ND * 2048], BF16)
        qt = sb("qt", [128, 2 * NQH * 512], BF16)
        kt = sb("kt", [128, 2 * 4096], BF16)
        vt = sb("vt", [128, 2 * 4096], BF16)
        oTu = sb("oTu", [128, NQH * 512], BF16)
        pb = sb("pb", [128, 2 * 1024], BF16)
        rsb = sb("rsb", [1, 512], BF16)
        rb = sb("rb", [128, 512], BF16)
        bbias = sb("bbias", [128, 16], F32)
        xres = sb("xres", [128, 2048], F32)
        anat = sb("anat", [128, 2048], BF16)
        tnorm = sb("tnorm", [128, 2 * 2048], F32)
        mv24 = sb("mv24", [128, 24], F32)
        mvc = sb("mvc", [128, 18], F32)
        psS = bs.enter_context(nc.psum_tensor(f"{P['tag']}_psS", [128, 2048], F32))
        psA = bs.enter_context(nc.psum_tensor(f"{P['tag']}_psA", [128, 512], F32))
        psU = bs.enter_context(nc.psum_tensor(f"{P['tag']}_psU", [128, 512], F32))
        psB = bs.enter_context(nc.psum_tensor(f"{P['tag']}_psB", [128, 512], F32))
        if has_bias:
            bob = sb("bob", [128, 2048], BF16)
            borow = tnorm
            borowb = pb
            bvcol = sb("bvcol", [128, 4], F32)
            bvcolb = sb("bvcolb", [128, 4], BF16)

        wo3 = wo[:, :].rearrange("p (d c) -> p d c", d=ND)
        qt3 = qt[:, :].rearrange("p (b h t) -> p b h t", b=2, h=NQH)
        kt2 = kt[:, :].rearrange("p (b t) -> p b t", b=2)
        vt3 = vt[:, :].rearrange("p (b c f) -> p b c f", b=2, f=128)
        oTu3 = oTu[:, :].rearrange("p (h t) -> p h t", h=NQH)
        oT3 = oTu3
        pb2 = pb[:, :].rearrange("p (b t) -> p b t", b=2)
        masks = G["masks"]
        ones_row = G["ones_row"]
        ones_col = G["ones_col"]

        s_w = ph.dsem("w")
        s_qt = ph.dsem("qt")
        s_kv = [ph.dsem("kv0"), ph.dsem("kv1")]
        kv_slot_cnt = [0, 0]
        s_xr = [ph.dsem("xr0"), ph.dsem("xr1")]
        xr_slot_cnt = [0, 0]
        s_rol = [ph.dsem("rol0"), ph.dsem("rol1")]
        ro_slot_cnt = [0, 0]
        s_sc = ph.sem("sc")
        s_xp = ph.sem("xp")
        s_mk = ph.sem("mk")
        s_p = ph.sem("p")
        s_av = ph.sem("av")
        s_oe = ph.sem("oe")
        s_rc = ph.sem("rc")
        s_bc = ph.sem("bc")
        s_no = ph.sem("no")
        s_op = ph.sem("op")
        s_an = ph.sem("an")
        s_mv = ph.sem("mv")
        s_sd = ph.sem("sd")
        s_ln = ph.sem("ln")
        s_t = ph.sem("t")
        s_x = ph.sem("x")
        s_ra = ph.sem("ra")
        s_ro = ph.dsem("ro")

        s_cl = ph.dsem("cl")
        _split_dma(ph, "gpsimd", wo3,
                   P["wo_ap"].rearrange("(d p) o -> p d o", p=128), ND, 8, s_w)
        def fn(e):
            return e.dma_start(out=bbias[:, :], in_=P["bbias_d"])
        ph.do("sync", fn, inc=s_cl, n=16)
        w_t = s_w.n

        cnt = dict(qt=0, kv=0, sc=0, p=0, av=0, rc=0, bc=0, no=0,
                   op=0, an=0, t=0, x=0, ra=0, ro=0, mk=0, oe=0)

        if has_bias:
            def fn(e):
                return e.dma_start(out=borow[0:1, 0:2048], in_=P["bo"].rearrange("(a o) -> a o", a=1))
            ph.do("sync", fn, inc=s_cl, n=16)
            def fn(e):
                return e.dma_start(out=bvcol[:, :], in_=P["bv"])
            ph.do("sync", fn, inc=s_cl, n=16)
            def fn(e):
                return e.tensor_copy(bvcolb[:, :], bvcol[:, :])
            ph.do("vector", fn, waits=[(s_cl, s_cl.n)], inc=s_rc)
            cnt["rc"] += 1
            for ntc in range(4):
                def fn(e, ntc=ntc):
                    r = None
                    for fc in range(ND):
                        r = e.matmul(psB[0:1, 0:512], bvcolb[:, fc // 4:fc // 4 + 1],
                                     wo3[:, fc, ntc * 512:(ntc + 1) * 512],
                                     start=(fc == 0), stop=(fc == ND - 1))
                    return r
                ph.do("tensor", fn, waits=[(s_rc, cnt["rc"]), (s_w, w_t)], inc=s_bc)
                cnt["bc"] += 1
                def fn(e, ntc=ntc):
                    return e.tensor_add(borow[0:1, ntc * 512:(ntc + 1) * 512],
                                        borow[0:1, ntc * 512:(ntc + 1) * 512],
                                        psB[0:1, 0:512])
                ph.do("vector", fn, waits=[(s_bc, cnt["bc"]), (s_cl, s_cl.n)], inc=s_rc)
                cnt["rc"] += 1
            def fn(e):
                return e.tensor_copy(borowb[0:1, 0:2048], borow[0:1, 0:2048])
            ph.do("vector", fn, waits=[(s_rc, cnt["rc"])], inc=s_rc)
            cnt["rc"] += 1
            for ntc in range(4):
                def fn(e, ntc=ntc):
                    return e.matmul(psS[:, ntc * 512:(ntc + 1) * 512], ones_row[0:1, :],
                                    borowb[0:1, ntc * 512:(ntc + 1) * 512],
                                    start=True, stop=True)
                ph.do("tensor", fn, waits=[(s_rc, cnt["rc"])], inc=s_bc)
                cnt["bc"] += 1
            def fn(e):
                return e.tensor_copy(bob[:, :], psS[:, 0:2048])
            ph.do("vector", fn, waits=[(s_bc, cnt["bc"])], inc=s_rc)
            cnt["rc"] += 1
        rc0 = cnt["rc"]
        cl_t = s_cl.n

        qb_end_sc = {}
        grp_end_av = {}
        gseq = 0
        for qb in range(NQB):
            nch = 4 * (qb + 1)
            npair = nch // 2
            qend = (qb + 1) * QB
            qsl = qb % 2
            waits = [(s_sc, qb_end_sc[qb - 2])] if qb >= 2 else []
            for part in range(4):
                def fn(e, part=part, qb=qb, qsl=qsl):
                    return e.dma_start(
                        out=qt3[:, qsl, part * 4:(part + 1) * 4, :],
                        in_=P["qT_s"].rearrange("(h p) t -> p h t", p=128)[
                            :, part * 4:(part + 1) * 4, qb * QB:(qb + 1) * QB])
                ph.do("sync", fn, waits=waits if part == 0 else [], inc=s_qt, n=16)
            cnt["qt"] += 1
            qt_t = 64 * cnt["qt"]

            for g in range(nkv):
                kslot = gseq % 2
                waits = [(s_av, grp_end_av[gseq - 2])] if gseq >= 2 else []
                def fn(e, kslot=kslot, g=g, qend=qend):
                    return e.dma_start(out=kt2[:, kslot, 0:qend],
                                       in_=P["kT_s"][g * 128:(g + 1) * 128, 0:qend])
                ph.do("sync", fn, waits=waits, inc=s_kv[kslot], n=16)
                waits = [(s_av, grp_end_av[gseq - 2])] if gseq >= 2 else []
                def fn(e, kslot=kslot, g=g, qend=qend, nch=nch):
                    return e.dma_start(out=vt3[:, kslot, 0:nch, :],
                                       in_=P["vT_s"][g * 128:(g + 1) * 128, 0:qend],
                                       transpose=True)
                ph.do("sync", fn, waits=waits, inc=s_kv[kslot], n=16)
                kv_slot_cnt[kslot] += 2
                kv_t = (kslot, 16 * kv_slot_cnt[kslot])

                for hh in range(hpg):
                    h = g * hpg + hh
                    pend_av = None
                    for pi in range(npair):
                        c0 = 2 * pi
                        gp = cnt["p"]
                        bank = gp % 2
                        pss = psS[:, bank * 1024:(bank + 1) * 1024]
                        waits = [(s_qt, qt_t), (s_kv[kv_t[0]], kv_t[1])]
                        if gp >= 2:
                            waits.append((s_p, gp - 1))
                        if qb == 0 and g == 0 and hh == 0 and pi == 0:
                            waits.append((s_rc, rc0))
                        if pi == 0 and g == 0 and hh == 0 and qb > 0:
                            waits.append((s_an, cnt["an"]))
                        def fn(e, pss=pss, kslot=kslot, h=h, c0=c0, qsl=qsl):
                            e.matmul(pss[:, 0:512],
                                     kt2[:, kslot, c0 * 128:(c0 + 1) * 128],
                                     qt3[:, qsl, h, :], start=True, stop=True)
                            return e.matmul(pss[:, 512:1024],
                                            kt2[:, kslot, (c0 + 1) * 128:(c0 + 2) * 128],
                                            qt3[:, qsl, h, :], start=True, stop=True)
                        ph.do("tensor", fn, waits=waits, inc=s_sc)
                        cnt["sc"] += 1
                        if pend_av is not None:
                            ph.do("tensor", pend_av[0], waits=pend_av[1], inc=s_av)
                            cnt["av"] += 1
                            pend_av = None

                        pslot = gp % 2
                        waits = [(s_sc, cnt["sc"]), (s_cl, cl_t)]
                        if gp >= 2:
                            waits.append((s_av, gp - 1))
                        def fx(e, pss=pss, pslot=pslot, h=h):
                            return e.activation(pb2[:, pslot, 0:1024], pss[:, 0:1024],
                                                AF.Exp, bias=bbias[:, h:h + 1], scale=ISQ)
                        mk_wait = []
                        if c0 + 2 > nch - 4:
                            ph.do("scalar", fx, waits=waits, inc=s_xp)
                            j0 = c0 - (nch - 4)
                            def fn(e, pslot=pslot, j0=j0):
                                return e.tensor_mul(
                                    pb2[:, pslot, 0:512], pb2[:, pslot, 0:512],
                                    masks[:, j0 * 512:(j0 + 1) * 512])
                            ph.do("vector", fn, waits=[(s_xp, s_xp.n)], inc=s_mk)
                            cnt["mk"] += 1
                            def fn(e, pslot=pslot, j0=j0):
                                return e.tensor_mul(
                                    pb2[:, pslot, 512:1024], pb2[:, pslot, 512:1024],
                                    masks[:, (j0 + 1) * 512:(j0 + 2) * 512])
                            ph.do("vector", fn, inc=s_p)
                            mk_wait = [(s_mk, cnt["mk"])]
                        else:
                            ph.do("scalar", fx, waits=waits, inc=s_p)
                        cnt["p"] += 1

                        waits = [(s_p, cnt["p"])] + mk_wait
                        if pi == 0:
                            waits.append((s_oe, cnt["oe"]))
                            waits.append((s_rc, cnt["rc"]))
                        def fn(e, pslot=pslot, kslot=kslot, c0=c0, nch=nch):
                            r = None
                            for cc in range(2):
                                c = c0 + cc
                                e.matmul(psA[:, 0:512], vt3[:, kslot, c, :],
                                         pb2[:, pslot, cc * 512:(cc + 1) * 512],
                                         start=(c == 0), stop=(c == nch - 1))
                            for cc in range(2):
                                c = c0 + cc
                                r = e.matmul(psU[0:1, 0:512], ones_col[:, :],
                                             pb2[:, pslot, cc * 512:(cc + 1) * 512],
                                             start=(c == 0), stop=(c == nch - 1))
                            return r
                        pend_av = (fn, waits)
                    ph.do("tensor", pend_av[0], waits=pend_av[1], inc=s_av)
                    cnt["av"] += 1

                    waits = [(s_av, cnt["av"])]
                    if qb >= 1:
                        waits.append((s_op, 4 * qb))
                    if cnt["no"] >= NQH:
                        waits.append((s_no, cnt["no"] - NQH + 1))
                    def fn(e, h=h):
                        return e.tensor_copy(oTu3[:, h, :], psA[:, 0:512])
                    ph.do("vector", fn, waits=waits, inc=s_oe)
                    cnt["oe"] += 1
                    def fn(e):
                        with nc.allow_low_precision(reason="bf16 softmax normalizer, "
                                                    "within tolerance"):
                            return e.reciprocal(rsb[0:1, :], psU[0:1, 0:512])
                    ph.do("vector", fn, waits=[(s_oe, cnt["oe"])], inc=s_rc)
                    cnt["rc"] += 1
                    def fn(e):
                        return e.matmul(psB[:, 0:512], ones_row[0:1, :], rsb[0:1, :],
                                        start=True, stop=True)
                    ph.do("tensor", fn,
                          waits=[(s_rc, cnt["rc"]), (s_no, cnt["no"])], inc=s_bc)
                    cnt["bc"] += 1
                    def fn(e):
                        return e.tensor_copy(rb[:, :], psB[:, 0:512])
                    ph.do("vector", fn, waits=[(s_bc, cnt["bc"])], inc=s_rc)
                    cnt["rc"] += 1
                    def fn(e, h=h):
                        return e.tensor_mul(oTu3[:, h, :], oTu3[:, h, :], rb[:, :])
                    ph.do("vector", fn, waits=[(s_rc, cnt["rc"])], inc=s_no)
                    cnt["no"] += 1
                grp_end_av[gseq] = cnt["av"]
                gseq += 1
            qb_end_sc[qb] = cnt["sc"]

            no_t = cnt["no"]
            if P.get("dbg_oT") is not None:
                def fn(e, qb=qb):
                    return e.dma_start(
                        out=P["dbg_oT"].rearrange("(h p) t -> p h t", p=128)[
                            :, :, qb * QB:(qb + 1) * QB],
                        in_=oT3[:, :, :])
                ph.do("gpsimd", fn, waits=[(s_no, no_t)], inc=s_ro, n=16)
                cnt["ro"] += 1
            for tc in range(4):
                r0 = qb * QB + tc * 128
                xs = 0
                waits = []
                if cnt["x"] >= 1:
                    waits.append((s_ra, cnt["x"]))
                def fn(e, r0=r0):
                    return e.dma_start(out=xres[:, 0:2048],
                                       in_=P["R_in"][r0:r0 + 128, :])
                ph.do("sync", fn, waits=waits, inc=s_xr[0], n=16)
                cnt["x"] += 1
                xr_slot_cnt[0] += 1

                waits = [(s_no, no_t)]
                if tc == 0:
                    waits.append((s_p, cnt["p"]))
                    if qb == 0:
                        waits.append((s_w, w_t))
                if cnt["op"] >= 1:
                    waits.append((s_an, cnt["an"]))
                def fn(e, tc=tc):
                    r = None
                    for ntc in range(4):
                        for fc in range(ND):
                            r = e.matmul(psS[:, ntc * 512:(ntc + 1) * 512],
                                         oT3[:, fc, tc * 128:(tc + 1) * 128],
                                         wo3[:, fc, ntc * 512:(ntc + 1) * 512],
                                         start=(fc == 0), stop=(fc == ND - 1))
                    return r
                ph.do("tensor", fn, waits=waits, inc=s_op)
                cnt["op"] += 1
                waits = [(s_op, cnt["op"])]
                if cnt["an"] >= 1:
                    waits.append((s_t, cnt["t"]))
                if has_bias:
                    def fn(e):
                        return e.scalar_tensor_tensor(anat[:, :],
                                                      psS[:, 0:2048], 1.0,
                                                      bob[:, :], ALU.mult, ALU.add)
                else:
                    def fn(e):
                        return e.tensor_copy(anat[:, :], psS[:, 0:2048])
                ph.do("vector", fn, waits=waits, inc=s_an)
                cnt["an"] += 1
                tsl = cnt["t"] % 2
                ow = [(s_rol[tsl], 16 * ro_slot_cnt[tsl])] if cnt["t"] >= 2 else []
                emit_ln(ph, anat[:, :],
                        tnorm[:, tsl * 2048:(tsl + 1) * 2048], mv24, mvc,
                        [(s_an, cnt["an"])], s_mv, s_sd, s_ln, s_t, out_waits=ow)
                cnt["t"] += 1
                def fn(e, tsl=tsl):
                    return e.tensor_add(tnorm[:, tsl * 2048:(tsl + 1) * 2048],
                                        tnorm[:, tsl * 2048:(tsl + 1) * 2048],
                                        xres[:, 0:2048])
                ph.do("vector", fn,
                      waits=[(s_t, cnt["t"]), (s_xr[0], 16 * xr_slot_cnt[0])],
                      inc=s_ra)
                cnt["ra"] += 1
                def fn(e, r0=r0, tsl=tsl):
                    return e.dma_start(out=P["R_out"][r0:r0 + 128, :],
                                       in_=tnorm[:, tsl * 2048:(tsl + 1) * 2048])
                ph.do("gpsimd", fn, waits=[(s_ra, cnt["ra"])], inc=s_rol[tsl], n=16)
                cnt["ro"] += 1
                ro_slot_cnt[tsl] += 1

        if P.get("dbg_num") is not None:
            def fn(e):
                return e.dma_start(out=P["dbg_num"][0:1, 0:512], in_=dbgU[0:1, 0:512])
            ph.do("gpsimd", fn, waits=[(s_rc, cnt["rc"])], inc=s_ro, n=16)
            cnt["ro"] += 1
        ph.do("gpsimd", lambda e: e.nop(),
              waits=[(s_rol[i], 16 * ro_slot_cnt[i]) for i in range(2)] +
                    ([(s_ro, 16 * cnt["ro"] // 1)] if False else []))
        ph.do("sync", lambda e: e.nop(),
              waits=[(s_xr[i], 16 * xr_slot_cnt[i]) for i in range(2)])
        ph.run()


def phase_ffnA(nc, stack, G, P):
    I = P["I"]
    NIB = I // 512
    TTF = P["TTF"]
    NTT = T // TTF
    NC = TTF // 128
    GPS = 4 * (TTF // 512)
    u8 = P["u8"]
    NXB, NHB = 3, 2

    with ExitStack() as bs:
        ph = Ph(nc, bs, P["tag"], pstack=stack)
        def sb(name, shape, dtype):
            return bs.enter_context(nc.sbuf_tensor(f"{P['tag']}_{name}", shape, dtype))
        xin = sb("xin", [128, NXB * 2048], F32)
        h_nat = sb("hnat", [128, NHB * 2048], BF16)
        hT = sb("hT", [128, ND * TTF], BF16)
        wg = sb("wg", [128, 2 * ND * 512], BF16)
        wu = sb("wu", [128, 2 * ND * 512], BF16)
        stg = sb("stg", [128, 2 * ND * 512], U8) if u8 else None
        stu = sb("stu", [128, 2 * ND * 512], U8) if u8 else None
        sig = sb("sig", [128, 2 * 512], BF16)
        hdn = sb("hdn", [128, 2 * TTF], BF16)
        gsc = sb("gsc", [128, I // 128], F32)
        usc = sb("usc", [128, I // 128], F32)
        mv24 = sb("mv24", [128, 24], F32)
        mvc = sb("mvc", [128, 18], F32)
        psT = bs.enter_context(nc.psum_tensor(f"{P['tag']}_psT", [128, 2048], F32))
        psP = bs.enter_context(nc.psum_tensor(f"{P['tag']}_psP", [128, 2048], F32))

        hT3 = hT[:, :].rearrange("p (d t) -> p d t", d=ND)
        wg4 = wg[:, :].rearrange("p (b d f) -> p b d f", b=2, d=ND)
        wu4 = wu[:, :].rearrange("p (b d f) -> p b d f", b=2, d=ND)
        if u8:
            stg4 = stg[:, :].rearrange("p (b d f) -> p b d f", b=2, d=ND)
            stu4 = stu[:, :].rearrange("p (b d f) -> p b d f", b=2, d=ND)
        ident = G["ident"]

        s_w = ph.dsem("w")
        s_wsl = [ph.dsem("wsl0"), ph.dsem("wsl1")]
        wsl_cnt = [0, 0]
        s_wc = ph.sem("wc")
        s_x = [ph.dsem(f"x{i}") for i in range(NXB)]
        x_slot_cnt = [0] * NXB
        s_hsp = [ph.dsem("hsp0"), ph.dsem("hsp1")]
        hsp_cnt = [0, 0]
        s_mv = ph.sem("mv")
        s_sd = ph.sem("sd")
        s_ln = ph.sem("ln")
        s_h = ph.sem("h")
        s_tr = ph.sem("tr")
        s_te = ph.sem("te")
        s_pq = ph.sem("pq")
        s_sg = ph.sem("sg")
        s_gs = ph.sem("gs")
        s_ev = ph.sem("ev")
        s_sp = ph.sem("sp")

        s_cl = ph.dsem("cl")
        def fn(e):
            return e.dma_start(out=gsc[:, :], in_=P["gs_ap"])
        ph.do("sync", fn, inc=s_cl, n=16)
        def fn(e):
            return e.dma_start(out=usc[:, :], in_=P["us_ap"])
        ph.do("sync", fn, inc=s_cl, n=16)
        sc_t = s_cl.n

        cnt = dict(x=0, h=0, te=0, wl=0, wc=0, pq=0, sg=0, ev=0, sp=0, gs=0)
        pq_end_prev_tile = 0

        for tt in range(NTT):
            for c in range(NC):
                ci = cnt["x"]
                xs = ci % NXB
                r0 = tt * TTF + c * 128
                waits = []
                if ci >= NXB:
                    waits.append((s_h, ci - NXB + 1))
                def fn(e, xs=xs, r0=r0):
                    return e.dma_start(out=xin[:, xs * 2048:(xs + 1) * 2048],
                                       in_=P["R_in"][r0:r0 + 128, :])
                ph.do("sync", fn, waits=waits, inc=s_x[xs], n=16)
                cnt["x"] += 1
                x_slot_cnt[xs] += 1
                hs = ci % NHB
                ow = []
                if ci >= NHB:
                    ow.append((s_tr, 2 * (ci - NHB) + 2))
                emit_ln(ph, xin[:, xs * 2048:(xs + 1) * 2048],
                        h_nat[:, hs * 2048:(hs + 1) * 2048],
                        mv24, mvc, [(s_x[xs], 16 * x_slot_cnt[xs])],
                        s_mv, s_sd, s_ln, s_h, out_waits=ow)
                cnt["h"] += 1
                for half in range(2):
                    tseq = 2 * ci + half
                    pst = psT[:, (tseq % 2) * 1024:((tseq % 2) + 1) * 1024]
                    waits = [(s_h, cnt["h"])]
                    if tseq >= 2:
                        waits.append((s_te, tseq - 1))
                    def fn(e, hs=hs, half=half, pst=pst):
                        r = None
                        for dd in range(8):
                            d = half * 8 + dd
                            r = e.matmul(pst[:, dd * 128:(dd + 1) * 128],
                                         h_nat[:, hs * 2048 + d * 128:hs * 2048 + (d + 1) * 128],
                                         ident[:, :], start=True, stop=True)
                        return r
                    ph.do("tensor", fn, waits=waits, inc=s_tr)
                    waits = [(s_tr, s_tr.n)]
                    if pq_end_prev_tile and cnt["te"] % (2 * NC) < 2:
                        waits.append((s_pq, pq_end_prev_tile))
                    def fn(e, pst=pst, half=half, c=c):
                        return e.copy(
                            hT3[:, half * 8:(half + 1) * 8, c * 128:(c + 1) * 128],
                            pst.rearrange("p (d t) -> p d t", d=8))
                    ph.do("scalar", fn, waits=waits, inc=s_te)
                    cnt["te"] += 1
            te_tile = cnt["te"]
            te_base = te_tile - 2 * NC

            for ib in range(NIB):
                wslot = cnt["wl"] % 2
                wl = cnt["wl"]
                guards = []
                if wl >= 2:
                    guards.append((s_pq, GPS * (wl - 1)))
                if u8:
                    if wl >= 2:
                        guards.append((s_wc, 2 * (wl - 1)))
                    for wsb, blk in ((stg4, P["wg_blk"]), (stu4, P["wu_blk"])):
                        for piece in range(2):
                            def fn(e, wsb=wsb, blk=blk, piece=piece, wslot=wslot, ib=ib):
                                return e.dma_start(
                                    out=wsb[:, wslot, piece * 8:(piece + 1) * 8, :],
                                    in_=blk[ib].rearrange("d p f -> p d f")[
                                        :, piece * 8:(piece + 1) * 8, :])
                            ph.do("sync", fn, waits=guards, inc=s_wsl[wslot], n=16)
                            guards = []
                    wsl_cnt[wslot] += 4
                    def fn(e, wslot=wslot):
                        return e.activation(wg4[:, wslot, :, :], stg4[:, wslot, :, :],
                                            AF.Copy, bias=-128.0)
                    ph.do("scalar", fn,
                          waits=[(s_wsl[wslot], 16 * wsl_cnt[wslot])], inc=s_wc)
                    cnt["wc"] += 1
                    def fn(e, wslot=wslot):
                        return e.activation(wu4[:, wslot, :, :], stu4[:, wslot, :, :],
                                            AF.Copy, bias=-128.0)
                    ph.do("scalar", fn, inc=s_wc)
                    cnt["wc"] += 1
                    w_ready = [(s_wc, cnt["wc"])]
                else:
                    for wsb, blk in ((wg4, P["wg_blk"]), (wu4, P["wu_blk"])):
                        for piece in range(2):
                            def fn(e, wsb=wsb, blk=blk, piece=piece, wslot=wslot, ib=ib):
                                return e.dma_start(
                                    out=wsb[:, wslot, piece * 8:(piece + 1) * 8, :],
                                    in_=blk[ib].rearrange("d p f -> p d f")[
                                        :, piece * 8:(piece + 1) * 8, :])
                            ph.do("gpsimd", fn, waits=guards, inc=s_wsl[wslot], n=16)
                            guards = []
                    wsl_cnt[wslot] += 4
                    w_ready = [(s_wsl[wslot], 16 * wsl_cnt[wslot])]
                cnt["wl"] += 1

                for icc in range(4):
                    ic = ib * 4 + icc
                    for nt in range(TTF // 512):
                        par = cnt["pq"] % 2
                        psg = psP[:, (2 * par) * 512:(2 * par + 1) * 512]
                        psu_ = psP[:, (2 * par + 1) * 512:(2 * par + 2) * 512]
                        waits = [(s_te, te_base + 8 * (nt + 1))] + w_ready
                        w_ready = []
                        if cnt["pq"] >= 2:
                            waits.append((s_ev, cnt["ev"] - 1))
                        def fn(e, icc=icc, nt=nt, wslot=wslot, psg=psg, psu_=psu_):
                            for d in range(ND):
                                e.matmul(psg[:, 0:512],
                                         wg4[:, wslot, d, icc * 128:(icc + 1) * 128],
                                         hT3[:, d, nt * 512:(nt + 1) * 512],
                                         start=(d == 0), stop=(d == ND - 1))
                            r = None
                            for d in range(ND):
                                r = e.matmul(psu_[:, 0:512],
                                             wu4[:, wslot, d, icc * 128:(icc + 1) * 128],
                                             hT3[:, d, nt * 512:(nt + 1) * 512],
                                             start=(d == 0), stop=(d == ND - 1))
                            return r
                        ph.do("tensor", fn, waits=waits, inc=s_pq)
                        cnt["pq"] += 1
                        ss = cnt["sg"] % 2
                        def fn(e, psg=psg, ss=ss, ic=ic):
                            return e.activation(sig[:, ss * 512:(ss + 1) * 512],
                                                psg[:, 0:512], AF.Silu,
                                                scale=gsc[:, ic:ic + 1])
                        ph.do("scalar", fn, waits=[(s_pq, cnt["pq"]), (s_cl, sc_t)],
                              inc=s_sg)
                        cnt["sg"] += 1
                        hs2 = (cnt["ev"] // (TTF // 512)) % 2
                        waits = [(s_sg, cnt["sg"])]
                        if nt == 0 and cnt["sp"] >= 2:
                            sl2 = cnt["sp"] % 2
                            waits.append((s_hsp[sl2], 16 * hsp_cnt[sl2]))
                        def fn(e, psu_=psu_, ss=ss, ic=ic, nt=nt, hs2=hs2):
                            return e.scalar_tensor_tensor(
                                hdn[:, hs2 * TTF + nt * 512:hs2 * TTF + (nt + 1) * 512],
                                psu_[:, 0:512], usc[:, ic:ic + 1],
                                sig[:, ss * 512:(ss + 1) * 512], ALU.mult, ALU.mult)
                        ph.do("vector", fn, waits=waits, inc=s_ev)
                        cnt["ev"] += 1
                    hs2 = cnt["sp"] % 2
                    def fn(e, ic=ic, tt=tt, hs2=hs2):
                        return e.dma_start(
                            out=P["hdn_s"][ic * 128:(ic + 1) * 128, tt * TTF:(tt + 1) * TTF],
                            in_=hdn[:, hs2 * TTF:(hs2 + 1) * TTF])
                    ph.do("gpsimd", fn, waits=[(s_ev, cnt["ev"])], inc=s_hsp[hs2], n=16)
                    cnt["sp"] += 1
                    hsp_cnt[hs2] += 1
            pq_end_prev_tile = cnt["pq"]

        ph.do("gpsimd", lambda e: e.nop(),
              waits=[(s_hsp[i], 16 * hsp_cnt[i]) for i in range(2)])
        ph.do("sync", lambda e: e.nop(),
              waits=[(s_x[i], 16 * x_slot_cnt[i]) for i in range(NXB)])
        ph.run()


def phase_ffnB(nc, stack, G, P):
    I = P["I"]
    NI = I // 128
    u8 = P["u8"]
    NT8 = T // 512

    with ExitStack() as bs:
        ph = Ph(nc, bs, P["tag"], pstack=stack)
        def sb(name, shape, dtype):
            return bs.enter_context(nc.sbuf_tensor(f"{P['tag']}_{name}", shape, dtype))
        wd = sb("wd", [128, NI * 1024], BF16)
        stw = sb("stw", [128, 8 * 1024], U8) if u8 else None
        hs = sb("hs", [128, 4 * 512], BF16)
        ob = sb("ob", [128, 2 * 8 * 512], BF16)
        dsc = sb("dsc", [128, 16], F32)
        wd3 = wd[:, :].rearrange("p (i h) -> p i h", i=NI)
        ob3 = ob[:, :].rearrange("p (b h t) -> p b h t", b=2, h=8)
        psO = bs.enter_context(nc.psum_tensor(f"{P['tag']}_psO", [128, 2048], F32))
        psO2 = bs.enter_context(nc.psum_tensor(f"{P['tag']}_psO2", [128, 2048], F32))

        s_w = ph.dsem("w")
        s_wc = ph.sem("wc")
        s_hl = [ph.dsem(f"hl{i}") for i in range(4)]
        hl_cnt = [0] * 4
        s_osp = [ph.dsem("osp0"), ph.dsem("osp1")]
        osp_cnt = [0, 0]
        s_ic = ph.sem("ic")
        s_ev = ph.sem("ev")
        s_sp = ph.sem("sp")

        s_cl = ph.dsem("cl")
        def fn(e):
            return e.dma_start(out=dsc[:, :], in_=P["ds_ap"])
        ph.do("sync", fn, inc=s_cl, n=16)
        ds_t = s_cl.n

        cnt = dict(wc=0, hl=0, ic=0, ev=0, sp=0, grp=0)

        for half in range(2):
            guards = [(s_ic, cnt["ic"])] if half > 0 else []
            if u8:
                for piece in range(NI // 8):
                    g2 = guards if piece == 0 else []
                    if cnt["wc"] >= 1:
                        g2 = g2 + [(s_wc, cnt["wc"])]
                    def fn(e, piece=piece, half=half):
                        return e.dma_start(
                            out=stw[:, :].rearrange("p (i h) -> p i h", i=8),
                            in_=P["wd_ap"].rearrange("(i p) h -> p i h", p=128)[
                                :, piece * 8:(piece + 1) * 8,
                                half * 1024:(half + 1) * 1024])
                    ph.do("sync", fn, waits=g2, inc=s_w, n=16)
                    def fn(e, piece=piece):
                        return e.activation(wd3[:, piece * 8:(piece + 1) * 8, :],
                                            stw[:, :].rearrange("p (i h) -> p i h", i=8),
                                            AF.Copy, bias=-128.0)
                    ph.do("scalar", fn, waits=[(s_w, s_w.n)], inc=s_wc)
                    cnt["wc"] += 1
                wsem, wtgt = s_wc, cnt["wc"]
            else:
                npiece = (NI + 3) // 4
                for piece in range(npiece):
                    i0, i1 = piece * 4, min(NI, piece * 4 + 4)
                    def fn(e, i0=i0, i1=i1, half=half):
                        return e.dma_start(
                            out=wd3[:, i0:i1, :],
                            in_=P["wd_ap"].rearrange("(i p) h -> p i h", p=128)[
                                :, i0:i1, half * 1024:(half + 1) * 1024])
                    ph.do("gpsimd", fn, waits=guards if piece == 0 else [], inc=s_w, n=16)
                wsem, wtgt = s_w, s_w.n

            for tt8 in range(NT8):
                for ic in range(NI):
                    hslot = cnt["hl"] % 4
                    waits = []
                    if cnt["hl"] >= 4:
                        waits.append((s_ic, cnt["hl"] - 3))
                    def fn(e, ic=ic, tt8=tt8, hslot=hslot):
                        return e.dma_start(
                            out=hs[:, hslot * 512:(hslot + 1) * 512],
                            in_=P["hdn_s"][ic * 128:(ic + 1) * 128,
                                           tt8 * 512:(tt8 + 1) * 512])
                    ph.do("sync", fn, waits=waits, inc=s_hl[hslot], n=16)
                    cnt["hl"] += 1
                    hl_cnt[hslot] += 1
                    waits = [(s_hl[hslot], 16 * hl_cnt[hslot]), (wsem, wtgt)]
                    if ic == 0:
                        waits.append((s_ev, cnt["ev"]))
                    def fn(e, ic=ic, hslot=hslot, NI=NI):
                        r = None
                        for hc in range(4):
                            r = e.matmul(psO[:, hc * 512:(hc + 1) * 512],
                                         wd3[:, ic, hc * 128:(hc + 1) * 128],
                                         hs[:, hslot * 512:(hslot + 1) * 512],
                                         start=(ic == 0), stop=(ic == NI - 1))
                        for hc in range(4):
                            r = e.matmul(psO2[:, hc * 512:(hc + 1) * 512],
                                         wd3[:, ic, (hc + 4) * 128:(hc + 5) * 128],
                                         hs[:, hslot * 512:(hslot + 1) * 512],
                                         start=(ic == 0), stop=(ic == NI - 1))
                        return r
                    ph.do("tensor", fn, waits=waits, inc=s_ic)
                    cnt["ic"] += 1

                oslot = cnt["grp"] % 2
                cnt["grp"] += 1
                sp_guard = [(s_osp[oslot], 16 * osp_cnt[oslot])] if cnt["sp"] >= 2 else []
                for hc in range(4):
                    def fn(e, oslot=oslot, half=half, hc=hc):
                        return e.activation(ob3[:, oslot, hc, :],
                                            psO[:, hc * 512:(hc + 1) * 512], AF.Identity,
                                            scale=dsc[:, half * 8 + hc:half * 8 + hc + 1])
                    w0 = [(s_ic, cnt["ic"]), (s_cl, ds_t)] + sp_guard if hc == 0 else []
                    ph.do("scalar", fn, waits=w0, inc=s_ev)
                    cnt["ev"] += 1
                for hc in range(4):
                    def fn(e, oslot=oslot, half=half, hc=hc):
                        return e.tensor_scalar(
                            ob3[:, oslot, 4 + hc, :], psO2[:, hc * 512:(hc + 1) * 512],
                            dsc[:, half * 8 + 4 + hc:half * 8 + 5 + hc], None, ALU.mult)
                    w0 = [(s_ic, cnt["ic"]), (s_cl, ds_t)] + sp_guard if hc == 0 else []
                    ph.do("vector", fn, waits=w0, inc=s_ev)
                    cnt["ev"] += 1
                def fn(e, oslot=oslot, half=half, tt8=tt8):
                    return e.dma_start(
                        out=P["oT_s"].rearrange("(h p) t -> p h t", p=128)[
                            :, half * 8:(half + 1) * 8, tt8 * 512:(tt8 + 1) * 512],
                        in_=ob3[:, oslot, :, :])
                ph.do("gpsimd", fn, waits=[(s_ev, cnt["ev"])], inc=s_osp[oslot], n=16)
                cnt["sp"] += 1
                osp_cnt[oslot] += 1

        ph.do("gpsimd", lambda e: e.nop(),
              waits=[(s_osp[i], 16 * osp_cnt[i]) for i in range(2)])
        ph.do("sync", lambda e: e.nop(),
              waits=[(s_hl[i], 16 * hl_cnt[i]) for i in range(4)])
        ph.run()


def phase_ffnT(nc, stack, G, P):
    with ExitStack() as bs:
        ph = Ph(nc, bs, P["tag"], pstack=stack)
        def sb(name, shape, dtype):
            return bs.enter_context(nc.sbuf_tensor(f"{P['tag']}_{name}", shape, dtype))
        ot = sb("ot", [128, 2 * ND * 512], BF16)
        xres = sb("xres", [128, 2 * 2048], F32)
        anat = sb("anat", [128, 2 * 2048], F32)
        tnorm = sb("tnorm", [128, 2 * 2048], F32)
        mv24 = sb("mv24", [128, 24], F32)
        mvc = sb("mvc", [128, 18], F32)
        ot4 = ot[:, :].rearrange("p (b d t) -> p b d t", b=2, d=ND)
        psT = bs.enter_context(nc.psum_tensor(f"{P['tag']}_psT", [128, 2048], F32))
        psT2 = bs.enter_context(nc.psum_tensor(f"{P['tag']}_psT2", [128, 2048], F32))
        ident = G["ident"]

        s_ol = [ph.dsem("ol0"), ph.dsem("ol1")]
        ol_cnt = [0, 0]
        s_xr = [ph.dsem("xr0"), ph.dsem("xr1")]
        xr_slot_cnt = [0, 0]
        s_rol = [ph.dsem("rol0"), ph.dsem("rol1")]
        ro_slot_cnt = [0, 0]
        s_tr = ph.sem("tr")
        s_ae = ph.sem("ae")
        s_mv = ph.sem("mv")
        s_sd = ph.sem("sd")
        s_ln = ph.sem("ln")
        s_t = ph.sem("t")
        s_x = ph.sem("x")
        s_ra = ph.sem("ra")
        s_ro = ph.sem("ro")

        cnt = dict(ol=0, tr=0, ae=0, t=0, x=0, ra=0, ro=0)
        for tg in range(T // 512):
            oslot = tg % 2
            waits = [(s_tr, 4 * (tg - 1))] if tg >= 2 else []
            for piece in range(2):
                def fn(e, oslot=oslot, tg=tg, piece=piece):
                    return e.dma_start(
                        out=ot4[:, oslot, piece * 8:(piece + 1) * 8, :],
                        in_=P["oT_s"].rearrange("(d p) t -> p d t", p=128)[
                            :, piece * 8:(piece + 1) * 8, tg * 512:(tg + 1) * 512])
                ph.do("sync", fn, waits=waits if piece == 0 else [], inc=s_ol[oslot], n=16)
            cnt["ol"] += 1
            ol_cnt[oslot] += 2

            for tc in range(4):
                gi = 4 * tg + tc
                pst = psT if gi % 2 == 0 else psT2
                r0 = gi * 128
                xs = gi % 2
                waits = []
                if gi >= 2:
                    waits.append((s_ra, gi - 1))
                def fn(e, xs=xs, r0=r0):
                    return e.dma_start(out=xres[:, xs * 2048:(xs + 1) * 2048],
                                       in_=P["R_in"][r0:r0 + 128, :])
                ph.do("sync", fn, waits=waits, inc=s_xr[xs], n=16)
                cnt["x"] += 1
                xr_slot_cnt[xs] += 1

                waits = [(s_ol[oslot], 16 * ol_cnt[oslot])]
                if gi >= 2:
                    waits.append((s_ae, gi - 1))
                def fn(e, oslot=oslot, tc=tc, pst=pst):
                    r = None
                    for d in range(ND):
                        r = e.matmul(pst[:, d * 128:(d + 1) * 128],
                                     ot4[:, oslot, d, tc * 128:(tc + 1) * 128],
                                     ident[:, :], start=True, stop=True)
                    return r
                ph.do("tensor", fn, waits=waits, inc=s_tr)
                cnt["tr"] += 1
                asl = gi % 2
                waits = [(s_tr, cnt["tr"])]
                if gi >= 2:
                    waits.append((s_t, cnt["t"] - 1))
                def fn(e, pst=pst, asl=asl):
                    return e.tensor_copy(anat[:, asl * 2048:(asl + 1) * 2048],
                                         pst[:, 0:2048])
                ph.do("vector", fn, waits=waits, inc=s_ae)
                cnt["ae"] += 1
                tsl = gi % 2
                ow = [(s_rol[tsl], 16 * ro_slot_cnt[tsl])] if cnt["t"] >= 2 else []
                emit_ln(ph, anat[:, asl * 2048:(asl + 1) * 2048],
                        tnorm[:, tsl * 2048:(tsl + 1) * 2048], mv24, mvc,
                        [(s_ae, cnt["ae"])], s_mv, s_sd, s_ln, s_t, out_waits=ow)
                cnt["t"] += 1
                def fn(e, xs=xs, tsl=tsl):
                    return e.tensor_add(tnorm[:, tsl * 2048:(tsl + 1) * 2048],
                                        tnorm[:, tsl * 2048:(tsl + 1) * 2048],
                                        xres[:, xs * 2048:(xs + 1) * 2048])
                ph.do("vector", fn,
                      waits=[(s_t, cnt["t"]), (s_xr[xs], 16 * xr_slot_cnt[xs])],
                      inc=s_ra)
                cnt["ra"] += 1
                def fn(e, r0=r0, tsl=tsl):
                    return e.dma_start(out=P["R_out"][r0:r0 + 128, :],
                                       in_=tnorm[:, tsl * 2048:(tsl + 1) * 2048])
                ph.do("gpsimd", fn, waits=[(s_ra, cnt["ra"])], inc=s_rol[tsl], n=16)
                cnt["ro"] += 1
                ro_slot_cnt[tsl] += 1

        ph.do("gpsimd", lambda e: e.nop(),
              waits=[(s_rol[i], 16 * ro_slot_cnt[i]) for i in range(2)])
        ph.do("sync", lambda e: e.nop(),
              waits=[(s_xr[i], 16 * xr_slot_cnt[i]) for i in range(2)])
        ph.run()


PHASELIST = ["qkv0", "attn0", "ffnA0", "ffnB0", "ffnT0",
             "qkv1", "attn1", "ffnA1", "ffnB1", "ffnT1"]
R_BEFORE = {"qkv0": "x", "attn0": "x", "ffnA0": "R1", "ffnB0": "R1", "ffnT0": "R1",
            "qkv1": "R2", "attn1": "R2", "ffnA1": "R3", "ffnB1": "R3", "ffnT1": "R3"}
R_PRODUCED = {"attn0": "R1", "ffnT0": "R2", "attn1": "R3", "ffnT1": "y"}
SCRATCH_PROD = {"qkv0": ["qT_s", "kT_s", "vT_s", "bb_d"],
                "qkv1": ["qT_s", "kT_s", "vT_s", "bb_d"],
                "ffnA0": ["hdn_s"], "ffnA1": ["hdn_s"],
                "ffnB0": ["oT_s"], "ffnB1": ["oT_s"]}
SCRATCH_CONS = {"attn0": ["qT_s", "kT_s", "vT_s", "bb_d"],
                "attn1": ["qT_s", "kT_s", "vT_s", "bb_d"],
                "ffnB0": ["hdn_s"], "ffnB1": ["hdn_s"],
                "ffnT0": ["oT_s"], "ffnT1": ["oT_s"]}
SCRATCH_SHAPES = {"qT_s": ([D, T], BF16), "kT_s": ([D, T], BF16),
                  "vT_s": ([D, T], BF16), "bb_d": ([128, 16], F32),
                  "hdn_s": ([I1, T], BF16), "oT_s": ([D, T], BF16)}


def build_nc(parts):
    nc = bass.Bass("TRN2", target_bir_lowering=False)
    dt_in = {}

    def din(name, shape, dtype):
        if name not in dt_in:
            dt_in[name] = nc.dram_tensor(name, shape, dtype, kind="ExternalInput").ap()
        return dt_in[name]

    rnames = set()
    for p in parts:
        rnames.add(R_BEFORE[p])
        if p in R_PRODUCED:
            rnames.add(R_PRODUCED[p])
    produced_r = {R_PRODUCED[p] for p in parts if p in R_PRODUCED}
    rt = {}
    for rn in rnames:
        if rn == "y":
            rt[rn] = nc.dram_tensor("y", [T, D], F32, kind="ExternalOutput").ap()
        elif rn == "x":
            rt[rn] = din("x", [T, D], F32)
        elif rn in produced_r:
            last = parts[-1]
            consumed_later = any(R_BEFORE[q] == rn for q in PHASELIST
                                 if q not in parts)
            kind = "ExternalOutput" if consumed_later else "Internal"
            rt[rn] = nc.dram_tensor(rn, [T, D], F32, kind=kind).ap()
        else:
            rt[rn] = din(rn, [T, D], F32)

    sc = {}
    need = set()
    for p in parts:
        for s in SCRATCH_PROD.get(p, []) + SCRATCH_CONS.get(p, []):
            need.add(s)
    for s in need:
        shape, dtype = SCRATCH_SHAPES[s]
        prod_in = any(s in SCRATCH_PROD.get(p, []) for p in parts)
        cons_in = any(s in SCRATCH_CONS.get(p, []) for p in parts)
        if prod_in and cons_in:
            kind = "Internal"
        elif prod_in:
            kind = "ExternalOutput"
        else:
            kind = "ExternalInput"
        if kind == "ExternalInput":
            sc[s] = din(s, shape, dtype)
        else:
            sc[s] = nc.dram_tensor(s, shape, dtype, kind=kind).ap()

    c_id = din("c_ident", [128, 128], BF16)
    c_or = din("c_onesrow", [1, 128], BF16)
    c_oc = din("c_onescol", [128, 1], BF16)
    c_mk = din("c_masks", [512, 512], BF16)
    c_cos = din("c_cos", [128, T], BF16)
    c_sin = din("c_sin", [128, T], BF16)

    w = {
        "l0_wq": din("l0_wq", [D, 2048], F32), "l0_wk": din("l0_wk", [D, 512], F32),
        "l0_wv": din("l0_wv", [D, 512], F32), "l0_wo": din("l0_wo", [D, 2048], F32),
        "l0_bq": din("l0_bq", [128, 16], F32), "l0_bk": din("l0_bk", [128, 4], F32),
        "l0_bv": din("l0_bv", [128, 4], F32), "l0_bo": din("l0_bo", [2048], F32),
        "l0_wg": din("l0_wg", [I0 // 512, ND, 128, 512], F32),
        "l0_wu": din("l0_wu", [I0 // 512, ND, 128, 512], F32),
        "l0_wd": din("l0_wd", [I0, D], F32),
        "l0_gs": din("l0_gs", [128, I0 // 128], F32),
        "l0_us": din("l0_us", [128, I0 // 128], F32),
        "l0_ds": din("l0_ds", [128, 16], F32),
        "l1_wp": din("l1_wp", [D, 6144], F32), "l1_wo": din("l1_wo", [D, 2048], F32),
        "l1_wg": din("l1_wg", [I1 // 512, ND, 128, 512], U8),
        "l1_wu": din("l1_wu", [I1 // 512, ND, 128, 512], U8),
        "l1_wd": din("l1_wd", [I1, D], U8),
        "l1_gs": din("l1_gs", [128, I1 // 128], F32),
        "l1_us": din("l1_us", [128, I1 // 128], F32),
        "l1_ds": din("l1_ds", [128, 16], F32),
    }

    with ExitStack() as stack:
        G = {}
        G["ident"] = stack.enter_context(nc.sbuf_tensor("g_id", [128, 128], BF16))
        G["ones_row"] = stack.enter_context(nc.sbuf_tensor("g_or", [1, 128], BF16))
        G["ones_col"] = stack.enter_context(nc.sbuf_tensor("g_oc", [128, 1], BF16))
        G["masks"] = stack.enter_context(nc.sbuf_tensor("g_mk", [128, 4 * 512], BF16))
        G["cos"] = stack.enter_context(nc.sbuf_tensor("g_cos", [128, T], BF16))
        G["sin"] = stack.enter_context(nc.sbuf_tensor("g_sin", [128, T], BF16))
        ph = Ph(nc, stack, "init")
        s_c = ph.sem("c")
        for dst, src in ((G["ident"], c_id), (G["ones_row"], c_or),
                         (G["ones_col"], c_oc), (G["cos"], c_cos), (G["sin"], c_sin)):
            def fn(e, dst=dst, src=src):
                return e.dma_start(out=dst[:, :], in_=src)
            ph.do("sync", fn, inc=s_c, n=16)
        def fn(e):
            return e.dma_start(out=G["masks"][:, :].rearrange("p (j t) -> p j t", j=4),
                               in_=c_mk.rearrange("(j p) t -> p j t", p=128))
        ph.do("sync", fn, inc=s_c, n=16)
        ph.do("sync", lambda e: e.nop(), waits=[(s_c, s_c.n)])
        ph.run()

        for p in parts:
            if p == "qkv0":
                phase_qkv(nc, stack, G, dict(
                    tag=p, rope=True, nkv=4, TTQ=1024,
                    wpasses=[[("q", 16, w["l0_wq"], 0)],
                             [("k", 4, w["l0_wk"], 0), ("v", 4, w["l0_wv"], 0)]],
                    R_in=rt["x"], bq=w["l0_bq"], bk=w["l0_bk"],
                    dst={"q": (sc["qT_s"], 0), "k": (sc["kT_s"], 0),
                         "v": (sc["vT_s"], 0)},
                    bbias_d=sc["bb_d"]))
            elif p == "qkv1":
                phase_qkv(nc, stack, G, dict(
                    tag=p, rope=False, nkv=16, TTQ=2048,
                    wpasses=[[("q", 16, w["l1_wp"], 0)],
                             [("k", 16, w["l1_wp"], 16)],
                             [("v", 16, w["l1_wp"], 32)]],
                    R_in=rt["R2"],
                    dst={"q": (sc["qT_s"], 0), "k": (sc["kT_s"], 0),
                         "v": (sc["vT_s"], 0)},
                    bbias_d=sc["bb_d"]))
            elif p == "attn0":
                dbg = None
                dbgn = None
                if DEBUG_OT:
                    dbg = nc.dram_tensor("dbg_oT", [D, T], BF16,
                                         kind="ExternalOutput").ap()
                    dbgn = nc.dram_tensor("dbg_num", [1, 512], F32,
                                          kind="ExternalOutput").ap()
                phase_attn(nc, stack, G, dict(
                    tag=p, rope=True, nkv=4, wo_ap=w["l0_wo"],
                    bo=w["l0_bo"], bv=w["l0_bv"], dbg_oT=dbg, dbg_num=dbgn,
                    qT_s=sc["qT_s"], kT_s=sc["kT_s"], vT_s=sc["vT_s"],
                    bbias_d=sc["bb_d"], R_in=rt["x"], R_out=rt["R1"]))
            elif p == "attn1":
                phase_attn(nc, stack, G, dict(
                    tag=p, rope=False, nkv=16, wo_ap=w["l1_wo"],
                    qT_s=sc["qT_s"], kT_s=sc["kT_s"], vT_s=sc["vT_s"],
                    bbias_d=sc["bb_d"], R_in=rt["R2"], R_out=rt["R3"]))
            elif p == "ffnA0":
                phase_ffnA(nc, stack, G, dict(
                    tag=p, I=I0, TTF=2048, u8=False,
                    wg_blk=w["l0_wg"], wu_blk=w["l0_wu"],
                    gs_ap=w["l0_gs"], us_ap=w["l0_us"],
                    R_in=rt["R1"], hdn_s=sc["hdn_s"]))
            elif p == "ffnA1":
                phase_ffnA(nc, stack, G, dict(
                    tag=p, I=I1, TTF=1024, u8=True,
                    wg_blk=w["l1_wg"], wu_blk=w["l1_wu"],
                    gs_ap=w["l1_gs"], us_ap=w["l1_us"],
                    R_in=rt["R3"], hdn_s=sc["hdn_s"]))
            elif p == "ffnB0":
                phase_ffnB(nc, stack, G, dict(
                    tag=p, I=I0, u8=False, wd_ap=w["l0_wd"], ds_ap=w["l0_ds"],
                    hdn_s=sc["hdn_s"], oT_s=sc["oT_s"]))
            elif p == "ffnB1":
                phase_ffnB(nc, stack, G, dict(
                    tag=p, I=I1, u8=True, wd_ap=w["l1_wd"], ds_ap=w["l1_ds"],
                    hdn_s=sc["hdn_s"], oT_s=sc["oT_s"]))
            elif p == "ffnT0":
                phase_ffnT(nc, stack, G, dict(
                    tag=p, oT_s=sc["oT_s"], R_in=rt["R1"], R_out=rt["R2"]))
            elif p == "ffnT1":
                phase_ffnT(nc, stack, G, dict(
                    tag=p, oT_s=sc["oT_s"], R_in=rt["R3"], R_out=rt["y"]))
    return nc


def _bf(x):
    return np.ascontiguousarray(x.astype(ml_dtypes.bfloat16))


def _host_consts():
    ident = np.eye(128, dtype=ml_dtypes.bfloat16)
    ones_row = np.ones((1, 128), dtype=ml_dtypes.bfloat16)
    ones_col = np.ones((128, 1), dtype=ml_dtypes.bfloat16)
    j = np.arange(4)[:, None, None]
    p = np.arange(128)[None, :, None]
    t = np.arange(512)[None, None, :]
    masks = (j * 128 + p <= t).astype(ml_dtypes.bfloat16).reshape(512, 512)
    d = HD
    i = np.arange(1, d // 2 + 1, dtype=np.float64)
    thetas = 10000.0 ** (-2.0 * (i - 1.0) / d)
    ang = np.arange(1, T + 1, dtype=np.float64)[:, None] * thetas
    cos = np.concatenate([np.cos(ang), np.cos(ang)], axis=1).T
    sin = np.concatenate([np.sin(ang), np.sin(ang)], axis=1).T
    sgn = np.ones((128, 1))
    sgn[:64] = -1.0
    return {"c_ident": ident, "c_onesrow": ones_row, "c_onescol": ones_col,
            "c_masks": np.ascontiguousarray(masks),
            "c_cos": _bf(cos), "c_sin": _bf(sin * sgn)}


def _pcol(v):
    return np.ascontiguousarray(np.asarray(v).reshape(-1, 128).T)


def _blkify(W, dtype):
    I = W.shape[0]
    a = W.reshape(I // 512, 512, ND, 128).transpose(0, 2, 3, 1)
    return np.ascontiguousarray(a).astype(dtype)


def _prep_weights(inputs):
    w = {}
    w["l0_wq"] = np.ascontiguousarray(inputs["L0_attn_Wq"].T)
    w["l0_wk"] = np.ascontiguousarray(inputs["L0_attn_Wk"].T)
    w["l0_wv"] = np.ascontiguousarray(inputs["L0_attn_Wv"].T)
    w["l0_wo"] = np.ascontiguousarray(inputs["L0_attn_Wo"].T)
    w["l0_bq"] = _pcol(inputs["L0_attn_bq"])
    w["l0_bk"] = _pcol(inputs["L0_attn_bk"])
    w["l0_bv"] = _pcol(inputs["L0_attn_bv"])
    w["l0_bo"] = inputs["L0_attn_bo"]
    w["l0_wg"] = _blkify(inputs["L0_ffn_gate_w"], np.float32)
    w["l0_wu"] = _blkify(inputs["L0_ffn_up_w"], np.float32)
    w["l0_wd"] = np.ascontiguousarray(inputs["L0_ffn_down_w"].T)
    w["l0_gs"] = _pcol(inputs["L0_ffn_gate_s"])
    w["l0_us"] = _pcol(inputs["L0_ffn_up_s"])
    w["l0_ds"] = _pcol(inputs["L0_ffn_down_s"])
    w["l1_wp"] = np.ascontiguousarray(inputs["L1_attn_w_project"].T)
    w["l1_wo"] = np.ascontiguousarray(inputs["L1_attn_w_out"].T)
    w["l1_wg"] = _blkify(inputs["L1_ffn_gate_w"].astype(np.uint8), np.uint8)
    w["l1_wu"] = _blkify(inputs["L1_ffn_up_w"].astype(np.uint8), np.uint8)
    w["l1_wd"] = np.ascontiguousarray(inputs["L1_ffn_down_w"].T.astype(np.uint8))
    w["l1_gs"] = _pcol(inputs["L1_ffn_gate_scale"])
    w["l1_us"] = _pcol(inputs["L1_ffn_up_scale"])
    w["l1_ds"] = _pcol(inputs["L1_ffn_down_scale"])
    return w


DEBUG_OT = False
LAUNCH_PLAN = [PHASELIST]


def kernel(**inputs):
    x = inputs["L0_attn_x"]
    w = _prep_weights(inputs)
    consts = _host_consts()
    base = dict(w)
    base.update(consts)

    shards = [np.ascontiguousarray(x[c]) for c in range(8)]
    carry = [dict() for _ in range(8)]
    out = np.empty((8, T, D), dtype=np.float32)
    for parts in LAUNCH_PLAN:
        nc = build_nc(parts)
        need = _declared_inputs(nc)
        in_maps = []
        for c in range(8):
            im = dict(base)
            im["x"] = shards[c]
            im.update(carry[c])
            im = {k: v for k, v in im.items() if k in need}
            in_maps.append(im)
        res = run_bass_kernel_spmd(nc, in_maps, core_ids=list(range(8)))
        for c in range(8):
            for k, v in res.results[c].items():
                if k == "y":
                    out[c] = v
                else:
                    carry[c][k] = v
    return out


def _declared_inputs(nc):
    names = set()
    for alloc in nc.m.functions[0].allocations:
        if isinstance(alloc, mybir.MemoryLocationSet) and alloc.kind == "ExternalInput":
            names.add(alloc.memorylocations[0].name)
    return names


# version 11
# speedup vs baseline: 1.1808x; 1.1808x over previous
import math
from contextlib import ExitStack

import numpy as np
import ml_dtypes

import concourse.bass as bass
import concourse.mybir as mybir
from concourse.bass_utils import run_bass_kernel_spmd

F32 = mybir.dt.float32
BF16 = mybir.dt.bfloat16
U8 = mybir.dt.uint8
AF = mybir.ActivationFunctionType
ALU = mybir.AluOpType
AX = mybir.AxisListType

T = 4096
D = 2048
ND = D // 128
HD = 128
NQH = 16
EPS = np.float32(1e-5)
ISQ = np.float32(1.0 / math.sqrt(HD))
QB = 512
NQB = T // QB
I0 = 5632
I1 = 8192

ENGINES = ("sync", "gpsimd", "scalar", "vector", "tensor")


class Sem:
    __slots__ = ("h", "n")

    def __init__(self, h):
        self.h = h
        self.n = 0


class Ph:
    def __init__(self, nc, stack, tag, pstack=None):
        self.nc = nc
        self.stack = stack
        self.pstack = pstack if pstack is not None else stack
        self.tag = tag
        self.ops = {e: [] for e in ENGINES}
        self.sems = []

    def sem(self, name):
        s = Sem(self.stack.enter_context(self.nc.semaphore(f"{self.tag}_{name}")))
        self.sems.append(s)
        return s

    def dsem(self, name):
        s = Sem(self.pstack.enter_context(
            self.nc.semaphore(f"{self.tag}_{name}")))
        self.sems.append(s)
        return s

    def do(self, eng, fn, waits=(), inc=None, n=1):
        w = [(s.h, t) for (s, t) in waits if t > 0]
        i = None
        if inc is not None:
            i = (inc.h, n)
            inc.n += n
        self.ops[eng].append((fn, w, i))

    def run(self):
        nc = self.nc

        def mk(lst):
            def body(eng):
                for fn, waits, inc in lst:
                    for sh, tgt in waits:
                        eng.wait_ge(sh, tgt)
                    inst = fn(eng)
                    if inc is not None:
                        inst.then_inc(inc[0], inc[1])
            return body

        sems = list(self.sems)
        if sems:
            with nc.Block(self.tag + "_clr") as blk:
                @blk.gpsimd
                def _(g):
                    for s in sems:
                        g.sem_clear(s.h)
        with nc.Block(self.tag) as blk:
            for e in ENGINES:
                if self.ops[e]:
                    getattr(blk, e)(mk(self.ops[e]))


def _split_dma(ph, queue, out_v, in_v, ndim0, per, sem, waits0=()):
    i = 0
    first = True
    while i < ndim0:
        j = min(ndim0, i + per)
        def fn(e, i=i, j=j):
            return e.dma_start(out=out_v[:, i:j], in_=in_v[:, i:j])
        ph.do(queue, fn, waits=waits0 if first else (), inc=sem, n=16)
        first = False
        i = j


def emit_ln(ph, xin_ap, out_ap, mv24, mvc, in_waits, s_mv, s_sd, s_ln, s_out,
            out_waits=()):
    def fn(e, xin_ap=xin_ap):
        r = None
        for k in range(4):
            r = e.bn_stats(mv24[:, k * 6:(k + 1) * 6], xin_ap[:, k * 512:(k + 1) * 512])
        return r
    ph.do("vector", fn, waits=in_waits, inc=s_mv)
    for k in range(4):
        def fn(e, k=k):
            return e.bn_aggr(mvc[:, k:k + 5:4], mv24[:, k * 6:(k + 1) * 6])
        ph.do("vector", fn, waits=[(s_mv, s_mv.n)] if k == 0 else [], inc=s_mv)

    def fn(e):
        e.reduce_sum(mvc[:, 8:9], mvc[:, 0:4], axis=AX.X)
        e.reduce_sum(mvc[:, 9:10], mvc[:, 4:8], axis=AX.X)
        return e.tensor_mul(mv24[:, 0:4], mvc[:, 0:4], mvc[:, 0:4])
    ph.do("vector", fn, waits=[(s_mv, s_mv.n)], inc=s_mv)

    def fn(e):
        e.reduce_sum(mvc[:, 10:11], mv24[:, 0:4], axis=AX.X)
        return e.tensor_scalar(mvc[:, 12:13], mvc[:, 8:9], 0.25, None, ALU.mult)
    ph.do("vector", fn, waits=[(s_mv, s_mv.n)], inc=s_mv)

    def fn(e):
        e.tensor_add(mvc[:, 11:12], mvc[:, 9:10], mvc[:, 10:11])
        return e.tensor_mul(mvc[:, 13:14], mvc[:, 12:13], mvc[:, 12:13])
    ph.do("vector", fn, waits=[(s_mv, s_mv.n)], inc=s_mv)

    def fn(e):
        return e.tensor_scalar(mvc[:, 14:15], mvc[:, 11:12], 0.25, None, ALU.mult)
    ph.do("vector", fn, waits=[(s_mv, s_mv.n)], inc=s_mv)

    def fn(e):
        return e.tensor_sub(mvc[:, 14:15], mvc[:, 14:15], mvc[:, 13:14])
    ph.do("vector", fn, waits=[(s_mv, s_mv.n)], inc=s_mv)

    def fn(e):
        return e.activation(mvc[:, 14:15], mvc[:, 14:15], AF.Sqrt, bias=EPS)
    ph.do("scalar", fn, waits=[(s_mv, s_mv.n)], inc=s_sd)

    def fn(e):
        return e.reciprocal(mvc[:, 15:16], mvc[:, 14:15])
    ph.do("vector", fn, waits=[(s_sd, s_sd.n)], inc=s_ln)

    def fn(e):
        return e.tensor_scalar(mvc[:, 16:17], mvc[:, 12:13], mvc[:, 15:16], -1.0,
                               ALU.mult, ALU.mult)
    ph.do("vector", fn, waits=[(s_ln, s_ln.n)], inc=s_ln)

    def fn(e, xin_ap=xin_ap, out_ap=out_ap):
        return e.activation(out_ap, xin_ap, AF.Identity,
                            bias=mvc[:, 16:17], scale=mvc[:, 15:16])
    ph.do("scalar", fn, waits=[(s_ln, s_ln.n)] + list(out_waits), inc=s_out)


def phase_qkv(nc, stack, G, P):
    TTQ = P["TTQ"]
    NTT = T // TTQ
    NC = TTQ // 128
    rope = P["rope"]
    nkv = P["nkv"]
    NXB, NHB, NQR, NSP = 3, 2, 2, 4

    with ExitStack() as bs:
        ph = Ph(nc, bs, P["tag"], pstack=stack)
        def sb(name, shape, dtype):
            return bs.enter_context(nc.sbuf_tensor(f"{P['tag']}_{name}", shape, dtype))
        xin = sb("xin", [128, NXB * 2048], F32)
        h_nat = sb("hnat", [128, NHB * 2048], BF16)
        hT = sb("hT", [128, ND * TTQ], BF16)
        wres = sb("wres", [128, ND * 2048], BF16)
        qraw = sb("qraw", [128, NQR * 512], BF16)
        tmpsw = sb("tmpsw", [128, NQR * 512], BF16)
        spq = sb("spq", [128, NSP * TTQ], BF16)
        sqb = sb("sqb", [128, 512], BF16)
        mv24 = sb("mv24", [128, 24], F32)
        mvc = sb("mvc", [128, 18], F32)
        norms = sb("norms", [1, 64], F32)
        biases = sb("biases", [128, 32], F32)
        brow = sb("brow", [1, 16], F32)
        browb = sb("browb", [1, 16], BF16)
        bbias = sb("bbias", [128, 16], F32)
        psT = bs.enter_context(nc.psum_tensor(f"{P['tag']}_psT", [128, 2048], F32))
        psP = bs.enter_context(nc.psum_tensor(f"{P['tag']}_psP", [128, 2048], F32))

        s_w = ph.dsem("w")
        s_x = [ph.dsem(f"x{i}") for i in range(NXB)]
        x_slot_cnt = [0] * NXB
        s_spl = [ph.dsem(f"spl{i}") for i in range(NSP)]
        sp_slot_cnt = [0] * NSP
        s_mv = ph.sem("mv")
        s_sd = ph.sem("sd")
        s_ln = ph.sem("ln")
        s_h = ph.sem("h")
        s_tr = ph.sem("tr")
        s_te = ph.sem("te")
        s_pq = ph.sem("pq")
        s_eb = ph.sem("eb")
        s_rp = ph.sem("rp")
        s_ev = ph.sem("ev")
        s_evs = ph.sem("evs")
        s_sq = ph.sem("sq")
        s_nm = ph.sem("nm")
        s_nr = ph.sem("nr")
        s_sp = ph.dsem("sp")
        s_bb = ph.sem("bb")

        hT3 = hT[:, :].rearrange("p (d t) -> p d t", d=ND)
        wres3 = wres[:, :].rearrange("p (d c) -> p d c", d=ND)
        ident = G["ident"]
        ones_row = G["ones_row"]
        ones_col = G["ones_col"]
        cosb = G["cos"]
        sinb = G["sin"]
        R = P["R_in"]

        s_cl = ph.dsem("cl")
        if rope:
            def fn(e):
                return e.dma_start(out=biases[:, 0:16], in_=P["bq"])
            ph.do("sync", fn, inc=s_cl, n=16)
            def fn(e):
                return e.dma_start(out=biases[:, 16:16 + nkv], in_=P["bk"])
            ph.do("sync", fn, inc=s_cl, n=16)
        bias_t = s_cl.n

        def fn(e):
            return e.memset(norms[0:1, :], 0.0)
        ph.do("vector", fn, inc=s_bb)

        cnt = dict(x=0, h=0, te=0, pq=0, ev=0, sq=0, nm=0, nr=0, sp=0, rp=0)
        ev_tokens = []
        rope_cnt = [0]
        slot_last_sp = [None] * NSP
        slot_last_sq = [None] * NSP
        pq_end_prev_tile = 0

        for wp, group in enumerate(P["wpasses"]):
            col0 = 0
            wguard = [(s_pq, cnt["pq"])] if cnt["pq"] > 0 else []
            for kind, noc, wap, ocbase in group:
                src = wap.rearrange("(d p) o -> p d o", p=128)[
                    :, :, ocbase * 128:(ocbase + noc) * 128]
                dstv = wres3[:, :, col0:col0 + noc * 128]
                _split_dma(ph, "gpsimd", dstv, src, ND, 8, s_w, waits0=wguard)
                wguard = []
                col0 += noc * 128
            w_t = s_w.n

            for tt in range(NTT):
                for c in range(NC):
                    ci = cnt["x"]
                    xs = ci % NXB
                    r0 = tt * TTQ + c * 128
                    waits = []
                    if ci >= NXB:
                        waits.append((s_h, ci - NXB + 1))
                    def fn(e, xs=xs, r0=r0):
                        return e.dma_start(out=xin[:, xs * 2048:(xs + 1) * 2048],
                                           in_=R[r0:r0 + 128, :])
                    ph.do("sync", fn, waits=waits, inc=s_x[xs], n=16)
                    cnt["x"] += 1
                    x_slot_cnt[xs] += 1
                    hs = ci % NHB
                    ow = []
                    if ci >= NHB:
                        ow.append((s_tr, 2 * (ci - NHB) + 2))
                    emit_ln(ph, xin[:, xs * 2048:(xs + 1) * 2048],
                            h_nat[:, hs * 2048:(hs + 1) * 2048],
                            mv24, mvc, [(s_x[xs], 16 * x_slot_cnt[xs])],
                            s_mv, s_sd, s_ln, s_h, out_waits=ow)
                    cnt["h"] += 1
                    for half in range(2):
                        tseq = 2 * ci + half
                        pst = psT[:, (tseq % 2) * 1024:((tseq % 2) + 1) * 1024]
                        waits = [(s_h, cnt["h"])]
                        if tseq >= 2:
                            waits.append((s_te, tseq - 1))
                        def fn(e, hs=hs, half=half, pst=pst):
                            r = None
                            for dd in range(8):
                                d = half * 8 + dd
                                r = e.matmul(pst[:, dd * 128:(dd + 1) * 128],
                                             h_nat[:, hs * 2048 + d * 128:hs * 2048 + (d + 1) * 128],
                                             ident[:, :], start=True, stop=True)
                            return r
                        ph.do("tensor", fn, waits=waits, inc=s_tr)
                        waits = [(s_tr, s_tr.n)]
                        if pq_end_prev_tile and cnt["te"] % (2 * NC) < 2:
                            waits.append((s_pq, pq_end_prev_tile))
                        def fn(e, pst=pst, half=half, c=c):
                            return e.copy(
                                hT3[:, half * 8:(half + 1) * 8, c * 128:(c + 1) * 128],
                                pst.rearrange("p (d t) -> p d t", d=8))
                        ph.do("scalar", fn, waits=waits, inc=s_te)
                        cnt["te"] += 1
                te_tile = cnt["te"]
                te_base = te_tile - 2 * NC

                colbase = 0
                for kind, noc, wap, ocbase in group:
                    for oc in range(noc):
                        spslot = cnt["sp"] % NSP
                        first_nt = True
                        for nt in range(TTQ // 512):
                            bank = cnt["pq"] % 2
                            pst = psP[:, bank * 512:(bank + 1) * 512]
                            waits = [(s_te, te_base + 8 * (nt + 1)), (s_w, w_t)]
                            if cnt["pq"] >= 2:
                                waits.append(ev_tokens[-2])
                            def fn(e, nt=nt, oc=oc, colbase=colbase, pst=pst):
                                r = None
                                for d in range(ND):
                                    r = e.matmul(pst[:, 0:512],
                                                 wres3[:, d, (colbase + oc) * 128:(colbase + oc + 1) * 128],
                                                 hT3[:, d, nt * 512:(nt + 1) * 512],
                                                 start=(d == 0), stop=(d == ND - 1))
                                return r
                            ph.do("tensor", fn, waits=waits, inc=s_pq)
                            cnt["pq"] += 1

                            if first_nt:
                                spill_guard = ([] if slot_last_sp[spslot] is None else
                                               [(s_spl[slot_last_sp[spslot][0]],
                                                 slot_last_sp[spslot][1])])
                                if slot_last_sq[spslot] is not None:
                                    spill_guard = spill_guard + [(s_sq, slot_last_sq[spslot])]
                            else:
                                spill_guard = []
                            first_nt = False
                            if rope and kind in ("q", "k"):
                                qs = cnt["ev"] % NQR
                                bcol = oc if kind == "q" else 16 + oc
                                waits = [(s_pq, cnt["pq"]), (s_cl, bias_t)] + spill_guard
                                if rope_cnt[0] >= NQR:
                                    waits.append((s_ev, rope_cnt[0] - NQR + 1))
                                def fn(e, qs=qs, bcol=bcol, pst=pst):
                                    return e.activation(qraw[:, qs * 512:(qs + 1) * 512],
                                                        pst[:, 0:512], AF.Identity,
                                                        bias=biases[:, bcol:bcol + 1], scale=np.float32(1.0))
                                ph.do("scalar", fn, waits=waits, inc=s_eb)
                                def fn(e, qs=qs, bcol=bcol, pst=pst):
                                    return e.activation(tmpsw[0:64, qs * 512:(qs + 1) * 512],
                                                        pst[64:128, 0:512], AF.Identity,
                                                        bias=biases[64:128, bcol:bcol + 1],
                                                        scale=np.float32(1.0))
                                ph.do("scalar", fn, inc=s_eb)
                                def fn(e, qs=qs, bcol=bcol, pst=pst):
                                    return e.activation(tmpsw[64:128, qs * 512:(qs + 1) * 512],
                                                        pst[0:64, 0:512], AF.Identity,
                                                        bias=biases[0:64, bcol:bcol + 1],
                                                        scale=np.float32(1.0))
                                ph.do("scalar", fn, inc=s_eb)
                                sw_t = s_eb.n
                                tc0 = tt * TTQ + nt * 512
                                def fn(e, qs=qs, tc0=tc0):
                                    return e.tensor_mul(qraw[:, qs * 512:(qs + 1) * 512],
                                                        qraw[:, qs * 512:(qs + 1) * 512],
                                                        cosb[:, tc0:tc0 + 512])
                                ph.do("vector", fn, waits=[(s_eb, sw_t)], inc=s_rp)
                                cnt["rp"] += 1
                                def fn(e, qs=qs, tc0=tc0):
                                    return e.tensor_mul(tmpsw[:, qs * 512:(qs + 1) * 512],
                                                        tmpsw[:, qs * 512:(qs + 1) * 512],
                                                        sinb[:, tc0:tc0 + 512])
                                ph.do("vector", fn, inc=s_rp)
                                cnt["rp"] += 1
                                def fn(e, qs=qs, spslot=spslot, nt=nt):
                                    return e.tensor_add(
                                        spq[:, spslot * TTQ + nt * 512:spslot * TTQ + (nt + 1) * 512],
                                        qraw[:, qs * 512:(qs + 1) * 512],
                                        tmpsw[:, qs * 512:(qs + 1) * 512])
                                ph.do("vector", fn, waits=[(s_rp, cnt["rp"])], inc=s_ev)
                                cnt["ev"] += 1
                                rope_cnt[0] += 1
                                ev_tokens.append((s_ev, s_ev.n))
                            else:
                                def fn(e, pst=pst, spslot=spslot, nt=nt):
                                    return e.copy(
                                        spq[:, spslot * TTQ + nt * 512:spslot * TTQ + (nt + 1) * 512],
                                        pst[:, 0:512])
                                ph.do("scalar", fn,
                                      waits=[(s_pq, cnt["pq"])] + spill_guard, inc=s_evs)
                                cnt["ev"] += 1
                                ev_tokens.append((s_evs, s_evs.n))

                            if kind in ("q", "k"):
                                ncx = oc if kind == "q" else 16 + oc
                                def fn(e, spslot=spslot, nt=nt):
                                    return e.tensor_mul(
                                        sqb[:, :],
                                        spq[:, spslot * TTQ + nt * 512:spslot * TTQ + (nt + 1) * 512],
                                        spq[:, spslot * TTQ + nt * 512:spslot * TTQ + (nt + 1) * 512])
                                ph.do("vector", fn, waits=[ev_tokens[-1]], inc=s_sq)
                                cnt["sq"] += 1
                                def fn(e):
                                    return e.matmul(psP[0:1, 1024:1536], ones_col[:, :],
                                                    sqb[:, :], start=True, stop=True)
                                ph.do("tensor", fn,
                                      waits=[(s_sq, cnt["sq"]), (s_nr, cnt["nr"])],
                                      inc=s_nm)
                                cnt["nm"] += 1
                                def fn(e):
                                    return e.reduce_max(norms[0:1, 63:64],
                                                        psP[0:1, 1024:1536], axis=AX.X)
                                ph.do("vector", fn, waits=[(s_nm, cnt["nm"])], inc=s_nr)
                                cnt["nr"] += 1
                                def fn(e, ncx=ncx):
                                    return e.tensor_max(norms[0:1, ncx:ncx + 1],
                                                        norms[0:1, ncx:ncx + 1],
                                                        norms[0:1, 63:64])
                                ph.do("vector", fn, waits=[(s_nr, cnt["nr"])], inc=s_nr)
                                cnt["nr"] += 1

                        dst_ap, row0 = P["dst"][kind]
                        col0 = tt * TTQ
                        def fn(e, dst_ap=dst_ap, row0=row0, oc=oc, spslot=spslot, col0=col0):
                            return e.dma_start(
                                out=dst_ap[(row0 + oc) * 128:(row0 + oc + 1) * 128,
                                           col0:col0 + TTQ],
                                in_=spq[:, spslot * TTQ:(spslot + 1) * TTQ])
                        ph.do("gpsimd", fn, waits=[ev_tokens[-1]],
                              inc=s_spl[spslot], n=16)
                        cnt["sp"] += 1
                        sp_slot_cnt[spslot] += 1
                        slot_last_sp[spslot] = (spslot, 16 * sp_slot_cnt[spslot])
                        slot_last_sq[spslot] = cnt["sq"] if kind in ("q", "k") else None
                    colbase += noc
                pq_end_prev_tile = cnt["pq"]

        for h in range(16):
            kcol = 16 + h // (16 // nkv)
            def fn(e, h=h, kcol=kcol):
                return e.tensor_tensor(norms[0:1, 32 + h:33 + h], norms[0:1, h:h + 1],
                                       norms[0:1, kcol:kcol + 1], ALU.mult)
            ph.do("vector", fn, waits=[(s_nr, cnt["nr"])] if h == 0 else [], inc=s_bb)
        def fn(e):
            return e.activation(norms[0:1, 32:48], norms[0:1, 32:48], AF.Sqrt)
        ph.do("scalar", fn, waits=[(s_bb, s_bb.n)], inc=s_bb)
        def fn(e):
            return e.mul(brow[0:1, 0:16], norms[0:1, 32:48], np.float32(-ISQ))
        ph.do("scalar", fn, waits=[(s_bb, s_bb.n)], inc=s_bb)
        def fn(e):
            return e.tensor_copy(browb[0:1, 0:16], brow[0:1, 0:16])
        ph.do("vector", fn, waits=[(s_bb, s_bb.n)], inc=s_bb)
        def fn(e):
            return e.matmul(psP[:, 1536:1552], ones_row[0:1, :], browb[0:1, 0:16],
                            start=True, stop=True)
        ph.do("tensor", fn, waits=[(s_bb, s_bb.n), (s_nm, cnt["nm"])], inc=s_bb)
        def fn(e):
            return e.tensor_copy(bbias[:, 0:16], psP[:, 1536:1552])
        ph.do("vector", fn, waits=[(s_bb, s_bb.n)], inc=s_bb)
        def fn(e):
            return e.dma_start(out=P["bbias_d"], in_=bbias[:, 0:16])
        ph.do("gpsimd", fn, waits=[(s_bb, s_bb.n)], inc=s_sp, n=16)
        ph.do("gpsimd", lambda e: e.nop(),
              waits=[(s_sp, 16)] + [(s_spl[i], 16 * sp_slot_cnt[i]) for i in range(NSP)])
        ph.do("sync", lambda e: e.nop(),
              waits=[(s_x[i], 16 * x_slot_cnt[i]) for i in range(NXB)])
        ph.run()


def phase_attn(nc, stack, G, P):
    nkv = P["nkv"]
    hpg = NQH // nkv
    has_bias = P["rope"]

    with ExitStack() as bs:
        ph = Ph(nc, bs, P["tag"], pstack=stack)
        def sb(name, shape, dtype):
            return bs.enter_context(nc.sbuf_tensor(f"{P['tag']}_{name}", shape, dtype))
        wo = sb("wo", [128, ND * 2048], BF16)
        qt = sb("qt", [128, 2 * NQH * 512], BF16)
        kt = sb("kt", [128, 2 * 4096], BF16)
        vt = sb("vt", [128, 2 * 4096], BF16)
        oTu = sb("oTu", [128, NQH * 512], BF16)
        pb = sb("pb", [128, 3 * 1024], BF16)
        rsb = sb("rsb", [1, 512], BF16)
        rb = sb("rb", [128, 512], BF16)
        bbias = sb("bbias", [128, 16], F32)
        xres = sb("xres", [128, 2048], F32)
        anat = sb("anat", [128, 2048], BF16)
        tnorm = sb("tnorm", [128, 2 * 2048], F32)
        mv24 = sb("mv24", [128, 24], F32)
        mvc = sb("mvc", [128, 18], F32)
        psS = bs.enter_context(nc.psum_tensor(f"{P['tag']}_psS", [128, 2048], F32))
        psA = bs.enter_context(nc.psum_tensor(f"{P['tag']}_psA", [128, 512], F32))
        psU = bs.enter_context(nc.psum_tensor(f"{P['tag']}_psU", [128, 512], F32))
        psB = bs.enter_context(nc.psum_tensor(f"{P['tag']}_psB", [128, 512], F32))
        if has_bias:
            bob = sb("bob", [128, 2048], BF16)
            borow = tnorm
            borowb = pb
            bvcol = sb("bvcol", [128, 4], F32)
            bvcolb = sb("bvcolb", [128, 4], BF16)

        wo3 = wo[:, :].rearrange("p (d c) -> p d c", d=ND)
        qt3 = qt[:, :].rearrange("p (b h t) -> p b h t", b=2, h=NQH)
        kt2 = kt[:, :].rearrange("p (b t) -> p b t", b=2)
        vt3 = vt[:, :].rearrange("p (b c f) -> p b c f", b=2, f=128)
        oTu3 = oTu[:, :].rearrange("p (h t) -> p h t", h=NQH)
        oT3 = oTu3
        pb2 = pb[:, :].rearrange("p (b t) -> p b t", b=3)
        masks = G["masks"]
        ones_row = G["ones_row"]
        ones_col = G["ones_col"]

        s_w = ph.dsem("w")
        s_qt = ph.dsem("qt")
        s_kv = [ph.dsem("kv0"), ph.dsem("kv1")]
        kv_slot_cnt = [0, 0]
        s_xr = [ph.dsem("xr0"), ph.dsem("xr1")]
        xr_slot_cnt = [0, 0]
        s_rol = [ph.dsem("rol0"), ph.dsem("rol1")]
        ro_slot_cnt = [0, 0]
        s_sc = ph.sem("sc")
        s_xp = ph.sem("xp")
        s_mk = ph.sem("mk")
        s_p = ph.sem("p")
        s_av = ph.sem("av")
        s_oe = ph.sem("oe")
        s_rc = ph.sem("rc")
        s_bc = ph.sem("bc")
        s_no = ph.sem("no")
        s_op = ph.sem("op")
        s_an = ph.sem("an")
        s_mv = ph.sem("mv")
        s_sd = ph.sem("sd")
        s_ln = ph.sem("ln")
        s_t = ph.sem("t")
        s_x = ph.sem("x")
        s_ra = ph.sem("ra")
        s_ro = ph.dsem("ro")

        s_cl = ph.dsem("cl")
        _split_dma(ph, "gpsimd", wo3,
                   P["wo_ap"].rearrange("(d p) o -> p d o", p=128), ND, 8, s_w)
        def fn(e):
            return e.dma_start(out=bbias[:, :], in_=P["bbias_d"])
        ph.do("sync", fn, inc=s_cl, n=16)
        w_t = s_w.n

        cnt = dict(qt=0, kv=0, sc=0, p=0, av=0, rc=0, bc=0, no=0,
                   op=0, an=0, t=0, x=0, ra=0, ro=0, mk=0, oe=0)

        if has_bias:
            def fn(e):
                return e.dma_start(out=borow[0:1, 0:2048], in_=P["bo"].rearrange("(a o) -> a o", a=1))
            ph.do("sync", fn, inc=s_cl, n=16)
            def fn(e):
                return e.dma_start(out=bvcol[:, :], in_=P["bv"])
            ph.do("sync", fn, inc=s_cl, n=16)
            def fn(e):
                return e.tensor_copy(bvcolb[:, :], bvcol[:, :])
            ph.do("vector", fn, waits=[(s_cl, s_cl.n)], inc=s_rc)
            cnt["rc"] += 1
            for ntc in range(4):
                def fn(e, ntc=ntc):
                    r = None
                    for fc in range(ND):
                        r = e.matmul(psB[0:1, 0:512], bvcolb[:, fc // 4:fc // 4 + 1],
                                     wo3[:, fc, ntc * 512:(ntc + 1) * 512],
                                     start=(fc == 0), stop=(fc == ND - 1))
                    return r
                ph.do("tensor", fn, waits=[(s_rc, cnt["rc"]), (s_w, w_t)], inc=s_bc)
                cnt["bc"] += 1
                def fn(e, ntc=ntc):
                    return e.tensor_add(borow[0:1, ntc * 512:(ntc + 1) * 512],
                                        borow[0:1, ntc * 512:(ntc + 1) * 512],
                                        psB[0:1, 0:512])
                ph.do("vector", fn, waits=[(s_bc, cnt["bc"]), (s_cl, s_cl.n)], inc=s_rc)
                cnt["rc"] += 1
            def fn(e):
                return e.tensor_copy(borowb[0:1, 0:2048], borow[0:1, 0:2048])
            ph.do("vector", fn, waits=[(s_rc, cnt["rc"])], inc=s_rc)
            cnt["rc"] += 1
            for ntc in range(4):
                def fn(e, ntc=ntc):
                    return e.matmul(psS[:, ntc * 512:(ntc + 1) * 512], ones_row[0:1, :],
                                    borowb[0:1, ntc * 512:(ntc + 1) * 512],
                                    start=True, stop=True)
                ph.do("tensor", fn, waits=[(s_rc, cnt["rc"])], inc=s_bc)
                cnt["bc"] += 1
            def fn(e):
                return e.tensor_copy(bob[:, :], psS[:, 0:2048])
            ph.do("vector", fn, waits=[(s_bc, cnt["bc"])], inc=s_rc)
            cnt["rc"] += 1
        rc0 = cnt["rc"]
        cl_t = s_cl.n

        qb_end_sc = {}
        grp_end_av = {}
        gseq = 0
        for qb in range(NQB):
            nch = 4 * (qb + 1)
            npair = nch // 2
            qend = (qb + 1) * QB
            qsl = qb % 2
            waits = [(s_sc, qb_end_sc[qb - 2])] if qb >= 2 else []
            for part in range(4):
                def fn(e, part=part, qb=qb, qsl=qsl):
                    return e.dma_start(
                        out=qt3[:, qsl, part * 4:(part + 1) * 4, :],
                        in_=P["qT_s"].rearrange("(h p) t -> p h t", p=128)[
                            :, part * 4:(part + 1) * 4, qb * QB:(qb + 1) * QB])
                ph.do("sync", fn, waits=waits if part == 0 else [], inc=s_qt, n=16)
            cnt["qt"] += 1
            qt_t = 64 * cnt["qt"]

            for g in range(nkv):
                kslot = gseq % 2
                waits = [(s_av, grp_end_av[gseq - 2])] if gseq >= 2 else []
                def fn(e, kslot=kslot, g=g, qend=qend):
                    return e.dma_start(out=kt2[:, kslot, 0:qend],
                                       in_=P["kT_s"][g * 128:(g + 1) * 128, 0:qend])
                ph.do("sync", fn, waits=waits, inc=s_kv[kslot], n=16)
                waits = [(s_av, grp_end_av[gseq - 2])] if gseq >= 2 else []
                def fn(e, kslot=kslot, g=g, qend=qend, nch=nch):
                    return e.dma_start(out=vt3[:, kslot, 0:nch, :],
                                       in_=P["vT_s"][g * 128:(g + 1) * 128, 0:qend],
                                       transpose=True)
                ph.do("scalar", fn, waits=waits, inc=s_kv[kslot], n=16)
                kv_slot_cnt[kslot] += 2
                kv_t = (kslot, 16 * kv_slot_cnt[kslot])

                for hh in range(hpg):
                    h = g * hpg + hh
                    pend_av = []
                    for pi in range(npair):
                        c0 = 2 * pi
                        gp = cnt["p"]
                        bank = gp % 2
                        pss = psS[:, bank * 1024:(bank + 1) * 1024]
                        waits = [(s_qt, qt_t), (s_kv[kv_t[0]], kv_t[1])]
                        if gp >= 2:
                            waits.append((s_p, gp - 1))
                        if qb == 0 and g == 0 and hh == 0 and pi == 0:
                            waits.append((s_rc, rc0))
                        if pi == 0 and g == 0 and hh == 0 and qb > 0:
                            waits.append((s_an, cnt["an"]))
                        def fn(e, pss=pss, kslot=kslot, h=h, c0=c0, qsl=qsl):
                            e.matmul(pss[:, 0:512],
                                     kt2[:, kslot, c0 * 128:(c0 + 1) * 128],
                                     qt3[:, qsl, h, :], start=True, stop=True)
                            return e.matmul(pss[:, 512:1024],
                                            kt2[:, kslot, (c0 + 1) * 128:(c0 + 2) * 128],
                                            qt3[:, qsl, h, :], start=True, stop=True)
                        ph.do("tensor", fn, waits=waits, inc=s_sc)
                        cnt["sc"] += 1
                        if len(pend_av) == 2:
                            f0, w0 = pend_av.pop(0)
                            ph.do("tensor", f0, waits=w0, inc=s_av)

                        pslot = gp % 3
                        waits = [(s_sc, cnt["sc"]), (s_cl, cl_t)]
                        if gp >= 3:
                            waits.append((s_av, gp - 2))
                        def fx(e, pss=pss, pslot=pslot, h=h):
                            return e.activation(pb2[:, pslot, 0:1024], pss[:, 0:1024],
                                                AF.Exp, bias=bbias[:, h:h + 1], scale=ISQ)
                        mk_wait = []
                        if c0 + 2 > nch - 4:
                            ph.do("scalar", fx, waits=waits, inc=s_xp)
                            j0 = c0 - (nch - 4)
                            def fn(e, pslot=pslot, j0=j0):
                                return e.tensor_mul(
                                    pb2[:, pslot, 0:512], pb2[:, pslot, 0:512],
                                    masks[:, j0 * 512:(j0 + 1) * 512])
                            ph.do("vector", fn, waits=[(s_xp, s_xp.n)], inc=s_mk)
                            cnt["mk"] += 1
                            def fn(e, pslot=pslot, j0=j0):
                                return e.tensor_mul(
                                    pb2[:, pslot, 512:1024], pb2[:, pslot, 512:1024],
                                    masks[:, (j0 + 1) * 512:(j0 + 2) * 512])
                            ph.do("vector", fn, inc=s_p)
                            mk_wait = [(s_mk, cnt["mk"])]
                        else:
                            ph.do("scalar", fx, waits=waits, inc=s_p)
                        cnt["p"] += 1

                        waits = [(s_p, cnt["p"])] + mk_wait
                        if pi == 0:
                            waits.append((s_oe, cnt["oe"]))
                            waits.append((s_rc, cnt["rc"]))
                        def fn(e, pslot=pslot, kslot=kslot, c0=c0, nch=nch):
                            r = None
                            for cc in range(2):
                                c = c0 + cc
                                e.matmul(psA[:, 0:512], vt3[:, kslot, c, :],
                                         pb2[:, pslot, cc * 512:(cc + 1) * 512],
                                         start=(c == 0), stop=(c == nch - 1))
                            for cc in range(2):
                                c = c0 + cc
                                r = e.matmul(psU[0:1, 0:512], ones_col[:, :],
                                             pb2[:, pslot, cc * 512:(cc + 1) * 512],
                                             start=(c == 0), stop=(c == nch - 1))
                            return r
                        pend_av.append((fn, waits))
                        cnt["av"] += 1
                    while pend_av:
                        f0, w0 = pend_av.pop(0)
                        ph.do("tensor", f0, waits=w0, inc=s_av)

                    waits = [(s_av, cnt["av"])]
                    if qb >= 1:
                        waits.append((s_op, 4 * qb))
                    if cnt["no"] >= NQH:
                        waits.append((s_no, cnt["no"] - NQH + 1))
                    def fn(e, h=h):
                        return e.tensor_copy(oTu3[:, h, :], psA[:, 0:512])
                    ph.do("vector", fn, waits=waits, inc=s_oe)
                    cnt["oe"] += 1
                    def fn(e):
                        with nc.allow_low_precision(reason="bf16 softmax normalizer, "
                                                    "within tolerance"):
                            return e.reciprocal(rsb[0:1, :], psU[0:1, 0:512])
                    ph.do("vector", fn, waits=[(s_oe, cnt["oe"])], inc=s_rc)
                    cnt["rc"] += 1
                    def fn(e):
                        return e.matmul(psB[:, 0:512], ones_row[0:1, :], rsb[0:1, :],
                                        start=True, stop=True)
                    ph.do("tensor", fn,
                          waits=[(s_rc, cnt["rc"]), (s_no, cnt["no"])], inc=s_bc)
                    cnt["bc"] += 1
                    def fn(e):
                        return e.tensor_copy(rb[:, :], psB[:, 0:512])
                    ph.do("vector", fn, waits=[(s_bc, cnt["bc"])], inc=s_rc)
                    cnt["rc"] += 1
                    def fn(e, h=h):
                        return e.tensor_mul(oTu3[:, h, :], oTu3[:, h, :], rb[:, :])
                    ph.do("vector", fn, waits=[(s_rc, cnt["rc"])], inc=s_no)
                    cnt["no"] += 1
                grp_end_av[gseq] = cnt["av"]
                gseq += 1
            qb_end_sc[qb] = cnt["sc"]

            no_t = cnt["no"]
            if P.get("dbg_oT") is not None:
                def fn(e, qb=qb):
                    return e.dma_start(
                        out=P["dbg_oT"].rearrange("(h p) t -> p h t", p=128)[
                            :, :, qb * QB:(qb + 1) * QB],
                        in_=oT3[:, :, :])
                ph.do("gpsimd", fn, waits=[(s_no, no_t)], inc=s_ro, n=16)
                cnt["ro"] += 1
            for tc in range(4):
                r0 = qb * QB + tc * 128
                xs = 0
                waits = []
                if cnt["x"] >= 1:
                    waits.append((s_ra, cnt["x"]))
                def fn(e, r0=r0):
                    return e.dma_start(out=xres[:, 0:2048],
                                       in_=P["R_in"][r0:r0 + 128, :])
                ph.do("sync", fn, waits=waits, inc=s_xr[0], n=16)
                cnt["x"] += 1
                xr_slot_cnt[0] += 1

                waits = [(s_no, no_t)]
                if tc == 0:
                    waits.append((s_p, cnt["p"]))
                    if qb == 0:
                        waits.append((s_w, w_t))
                if cnt["op"] >= 1:
                    waits.append((s_an, cnt["an"]))
                def fn(e, tc=tc):
                    r = None
                    for ntc in range(4):
                        for fc in range(ND):
                            r = e.matmul(psS[:, ntc * 512:(ntc + 1) * 512],
                                         oT3[:, fc, tc * 128:(tc + 1) * 128],
                                         wo3[:, fc, ntc * 512:(ntc + 1) * 512],
                                         start=(fc == 0), stop=(fc == ND - 1))
                    return r
                ph.do("tensor", fn, waits=waits, inc=s_op)
                cnt["op"] += 1
                waits = [(s_op, cnt["op"])]
                if cnt["an"] >= 1:
                    waits.append((s_t, cnt["t"]))
                if has_bias:
                    def fn(e):
                        return e.scalar_tensor_tensor(anat[:, :],
                                                      psS[:, 0:2048], 1.0,
                                                      bob[:, :], ALU.mult, ALU.add)
                else:
                    def fn(e):
                        return e.tensor_copy(anat[:, :], psS[:, 0:2048])
                ph.do("vector", fn, waits=waits, inc=s_an)
                cnt["an"] += 1
                tsl = cnt["t"] % 2
                ow = [(s_rol[tsl], 16 * ro_slot_cnt[tsl])] if cnt["t"] >= 2 else []
                emit_ln(ph, anat[:, :],
                        tnorm[:, tsl * 2048:(tsl + 1) * 2048], mv24, mvc,
                        [(s_an, cnt["an"])], s_mv, s_sd, s_ln, s_t, out_waits=ow)
                cnt["t"] += 1
                def fn(e, tsl=tsl):
                    return e.tensor_add(tnorm[:, tsl * 2048:(tsl + 1) * 2048],
                                        tnorm[:, tsl * 2048:(tsl + 1) * 2048],
                                        xres[:, 0:2048])
                ph.do("vector", fn,
                      waits=[(s_t, cnt["t"]), (s_xr[0], 16 * xr_slot_cnt[0])],
                      inc=s_ra)
                cnt["ra"] += 1
                def fn(e, r0=r0, tsl=tsl):
                    return e.dma_start(out=P["R_out"][r0:r0 + 128, :],
                                       in_=tnorm[:, tsl * 2048:(tsl + 1) * 2048])
                ph.do("gpsimd", fn, waits=[(s_ra, cnt["ra"])], inc=s_rol[tsl], n=16)
                cnt["ro"] += 1
                ro_slot_cnt[tsl] += 1

        if P.get("dbg_num") is not None:
            def fn(e):
                return e.dma_start(out=P["dbg_num"][0:1, 0:512], in_=dbgU[0:1, 0:512])
            ph.do("gpsimd", fn, waits=[(s_rc, cnt["rc"])], inc=s_ro, n=16)
            cnt["ro"] += 1
        ph.do("gpsimd", lambda e: e.nop(),
              waits=[(s_rol[i], 16 * ro_slot_cnt[i]) for i in range(2)] +
                    ([(s_ro, 16 * cnt["ro"] // 1)] if False else []))
        ph.do("sync", lambda e: e.nop(),
              waits=[(s_xr[i], 16 * xr_slot_cnt[i]) for i in range(2)])
        ph.run()


def phase_ffnA(nc, stack, G, P):
    I = P["I"]
    NIB = I // 512
    TTF = P["TTF"]
    NTT = T // TTF
    NC = TTF // 128
    GPS = 4 * (TTF // 512)
    u8 = P["u8"]
    NXB, NHB = 3, 2

    with ExitStack() as bs:
        ph = Ph(nc, bs, P["tag"], pstack=stack)
        def sb(name, shape, dtype):
            return bs.enter_context(nc.sbuf_tensor(f"{P['tag']}_{name}", shape, dtype))
        xin = sb("xin", [128, NXB * 2048], F32)
        h_nat = sb("hnat", [128, NHB * 2048], BF16)
        hT = sb("hT", [128, ND * TTF], BF16)
        wg = sb("wg", [128, 2 * ND * 512], BF16)
        wu = sb("wu", [128, 2 * ND * 512], BF16)
        stg = sb("stg", [128, 2 * ND * 512], U8) if u8 else None
        stu = sb("stu", [128, 2 * ND * 512], U8) if u8 else None
        sig = sb("sig", [128, 2 * 512], BF16)
        hdn = sb("hdn", [128, 2 * TTF], BF16)
        gsc = sb("gsc", [128, I // 128], F32)
        usc = sb("usc", [128, I // 128], F32)
        mv24 = sb("mv24", [128, 24], F32)
        mvc = sb("mvc", [128, 18], F32)
        psT = bs.enter_context(nc.psum_tensor(f"{P['tag']}_psT", [128, 2048], F32))
        psP = bs.enter_context(nc.psum_tensor(f"{P['tag']}_psP", [128, 2048], F32))

        hT3 = hT[:, :].rearrange("p (d t) -> p d t", d=ND)
        wg4 = wg[:, :].rearrange("p (b d f) -> p b d f", b=2, d=ND)
        wu4 = wu[:, :].rearrange("p (b d f) -> p b d f", b=2, d=ND)
        if u8:
            stg4 = stg[:, :].rearrange("p (b d f) -> p b d f", b=2, d=ND)
            stu4 = stu[:, :].rearrange("p (b d f) -> p b d f", b=2, d=ND)
        ident = G["ident"]

        s_w = ph.dsem("w")
        s_wsl = [ph.dsem("wsl0"), ph.dsem("wsl1")]
        wsl_cnt = [0, 0]
        s_wc = ph.sem("wc")
        s_x = [ph.dsem(f"x{i}") for i in range(NXB)]
        x_slot_cnt = [0] * NXB
        s_hsp = [ph.dsem("hsp0"), ph.dsem("hsp1")]
        hsp_cnt = [0, 0]
        s_mv = ph.sem("mv")
        s_sd = ph.sem("sd")
        s_ln = ph.sem("ln")
        s_h = ph.sem("h")
        s_tr = ph.sem("tr")
        s_te = ph.sem("te")
        s_pq = ph.sem("pq")
        s_sg = ph.sem("sg")
        s_gs = ph.sem("gs")
        s_ev = ph.sem("ev")
        s_sp = ph.sem("sp")

        s_cl = ph.dsem("cl")
        def fn(e):
            return e.dma_start(out=gsc[:, :], in_=P["gs_ap"])
        ph.do("sync", fn, inc=s_cl, n=16)
        def fn(e):
            return e.dma_start(out=usc[:, :], in_=P["us_ap"])
        ph.do("sync", fn, inc=s_cl, n=16)
        sc_t = s_cl.n

        cnt = dict(x=0, h=0, te=0, wl=0, wc=0, pq=0, sg=0, ev=0, sp=0, gs=0)
        pq_end_prev_tile = 0

        for tt in range(NTT):
            for c in range(NC):
                ci = cnt["x"]
                xs = ci % NXB
                r0 = tt * TTF + c * 128
                waits = []
                if ci >= NXB:
                    waits.append((s_h, ci - NXB + 1))
                def fn(e, xs=xs, r0=r0):
                    return e.dma_start(out=xin[:, xs * 2048:(xs + 1) * 2048],
                                       in_=P["R_in"][r0:r0 + 128, :])
                ph.do("sync", fn, waits=waits, inc=s_x[xs], n=16)
                cnt["x"] += 1
                x_slot_cnt[xs] += 1
                hs = ci % NHB
                ow = []
                if ci >= NHB:
                    ow.append((s_tr, 2 * (ci - NHB) + 2))
                emit_ln(ph, xin[:, xs * 2048:(xs + 1) * 2048],
                        h_nat[:, hs * 2048:(hs + 1) * 2048],
                        mv24, mvc, [(s_x[xs], 16 * x_slot_cnt[xs])],
                        s_mv, s_sd, s_ln, s_h, out_waits=ow)
                cnt["h"] += 1
                for half in range(2):
                    tseq = 2 * ci + half
                    pst = psT[:, (tseq % 2) * 1024:((tseq % 2) + 1) * 1024]
                    waits = [(s_h, cnt["h"])]
                    if tseq >= 2:
                        waits.append((s_te, tseq - 1))
                    def fn(e, hs=hs, half=half, pst=pst):
                        r = None
                        for dd in range(8):
                            d = half * 8 + dd
                            r = e.matmul(pst[:, dd * 128:(dd + 1) * 128],
                                         h_nat[:, hs * 2048 + d * 128:hs * 2048 + (d + 1) * 128],
                                         ident[:, :], start=True, stop=True)
                        return r
                    ph.do("tensor", fn, waits=waits, inc=s_tr)
                    waits = [(s_tr, s_tr.n)]
                    if pq_end_prev_tile and cnt["te"] % (2 * NC) < 2:
                        waits.append((s_pq, pq_end_prev_tile))
                    def fn(e, pst=pst, half=half, c=c):
                        return e.copy(
                            hT3[:, half * 8:(half + 1) * 8, c * 128:(c + 1) * 128],
                            pst.rearrange("p (d t) -> p d t", d=8))
                    ph.do("scalar", fn, waits=waits, inc=s_te)
                    cnt["te"] += 1
            te_tile = cnt["te"]
            te_base = te_tile - 2 * NC

            for ib in range(NIB):
                wslot = cnt["wl"] % 2
                wl = cnt["wl"]
                guards = []
                if wl >= 2:
                    guards.append((s_pq, GPS * (wl - 1)))
                if u8:
                    if wl >= 2:
                        guards.append((s_wc, 2 * (wl - 1)))
                    for wsb, blk in ((stg4, P["wg_blk"]), (stu4, P["wu_blk"])):
                        for piece in range(2):
                            def fn(e, wsb=wsb, blk=blk, piece=piece, wslot=wslot, ib=ib):
                                return e.dma_start(
                                    out=wsb[:, wslot, piece * 8:(piece + 1) * 8, :],
                                    in_=blk[ib].rearrange("d p f -> p d f")[
                                        :, piece * 8:(piece + 1) * 8, :])
                            ph.do("sync", fn, waits=guards, inc=s_wsl[wslot], n=16)
                            guards = []
                    wsl_cnt[wslot] += 4
                    def fn(e, wslot=wslot):
                        return e.activation(wg4[:, wslot, :, :], stg4[:, wslot, :, :],
                                            AF.Copy, bias=-128.0)
                    ph.do("scalar", fn,
                          waits=[(s_wsl[wslot], 16 * wsl_cnt[wslot])], inc=s_wc)
                    cnt["wc"] += 1
                    def fn(e, wslot=wslot):
                        return e.activation(wu4[:, wslot, :, :], stu4[:, wslot, :, :],
                                            AF.Copy, bias=-128.0)
                    ph.do("scalar", fn, inc=s_wc)
                    cnt["wc"] += 1
                    w_ready = [(s_wc, cnt["wc"])]
                else:
                    for wsb, blk in ((wg4, P["wg_blk"]), (wu4, P["wu_blk"])):
                        for piece in range(2):
                            def fn(e, wsb=wsb, blk=blk, piece=piece, wslot=wslot, ib=ib):
                                return e.dma_start(
                                    out=wsb[:, wslot, piece * 8:(piece + 1) * 8, :],
                                    in_=blk[ib].rearrange("d p f -> p d f")[
                                        :, piece * 8:(piece + 1) * 8, :])
                            ph.do("gpsimd", fn, waits=guards, inc=s_wsl[wslot], n=16)
                            guards = []
                    wsl_cnt[wslot] += 4
                    w_ready = [(s_wsl[wslot], 16 * wsl_cnt[wslot])]
                cnt["wl"] += 1

                for icc in range(4):
                    ic = ib * 4 + icc
                    for nt in range(TTF // 512):
                        par = cnt["pq"] % 2
                        psg = psP[:, (2 * par) * 512:(2 * par + 1) * 512]
                        psu_ = psP[:, (2 * par + 1) * 512:(2 * par + 2) * 512]
                        waits = [(s_te, te_base + 8 * (nt + 1))] + w_ready
                        w_ready = []
                        if cnt["pq"] >= 2:
                            waits.append((s_ev, cnt["ev"] - 1))
                        def fn(e, icc=icc, nt=nt, wslot=wslot, psg=psg, psu_=psu_):
                            for d in range(ND):
                                e.matmul(psg[:, 0:512],
                                         wg4[:, wslot, d, icc * 128:(icc + 1) * 128],
                                         hT3[:, d, nt * 512:(nt + 1) * 512],
                                         start=(d == 0), stop=(d == ND - 1))
                            r = None
                            for d in range(ND):
                                r = e.matmul(psu_[:, 0:512],
                                             wu4[:, wslot, d, icc * 128:(icc + 1) * 128],
                                             hT3[:, d, nt * 512:(nt + 1) * 512],
                                             start=(d == 0), stop=(d == ND - 1))
                            return r
                        ph.do("tensor", fn, waits=waits, inc=s_pq)
                        cnt["pq"] += 1
                        ss = cnt["sg"] % 2
                        def fn(e, psg=psg, ss=ss, ic=ic):
                            return e.activation(sig[:, ss * 512:(ss + 1) * 512],
                                                psg[:, 0:512], AF.Silu,
                                                scale=gsc[:, ic:ic + 1])
                        ph.do("scalar", fn, waits=[(s_pq, cnt["pq"]), (s_cl, sc_t)],
                              inc=s_sg)
                        cnt["sg"] += 1
                        hs2 = (cnt["ev"] // (TTF // 512)) % 2
                        waits = [(s_sg, cnt["sg"])]
                        if nt == 0 and cnt["sp"] >= 2:
                            sl2 = cnt["sp"] % 2
                            waits.append((s_hsp[sl2], 16 * hsp_cnt[sl2]))
                        def fn(e, psu_=psu_, ss=ss, ic=ic, nt=nt, hs2=hs2):
                            return e.scalar_tensor_tensor(
                                hdn[:, hs2 * TTF + nt * 512:hs2 * TTF + (nt + 1) * 512],
                                psu_[:, 0:512], usc[:, ic:ic + 1],
                                sig[:, ss * 512:(ss + 1) * 512], ALU.mult, ALU.mult)
                        ph.do("vector", fn, waits=waits, inc=s_ev)
                        cnt["ev"] += 1
                    hs2 = cnt["sp"] % 2
                    def fn(e, ic=ic, tt=tt, hs2=hs2):
                        return e.dma_start(
                            out=P["hdn_s"][ic * 128:(ic + 1) * 128, tt * TTF:(tt + 1) * TTF],
                            in_=hdn[:, hs2 * TTF:(hs2 + 1) * TTF])
                    ph.do("gpsimd", fn, waits=[(s_ev, cnt["ev"])], inc=s_hsp[hs2], n=16)
                    cnt["sp"] += 1
                    hsp_cnt[hs2] += 1
            pq_end_prev_tile = cnt["pq"]

        ph.do("gpsimd", lambda e: e.nop(),
              waits=[(s_hsp[i], 16 * hsp_cnt[i]) for i in range(2)])
        ph.do("sync", lambda e: e.nop(),
              waits=[(s_x[i], 16 * x_slot_cnt[i]) for i in range(NXB)])
        ph.run()


def phase_ffnB(nc, stack, G, P):
    I = P["I"]
    NI = I // 128
    u8 = P["u8"]
    NT8 = T // 512

    with ExitStack() as bs:
        ph = Ph(nc, bs, P["tag"], pstack=stack)
        def sb(name, shape, dtype):
            return bs.enter_context(nc.sbuf_tensor(f"{P['tag']}_{name}", shape, dtype))
        wd = sb("wd", [128, NI * 1024], BF16)
        stw = sb("stw", [128, 8 * 1024], U8) if u8 else None
        hs = sb("hs", [128, 4 * 512], BF16)
        ob = sb("ob", [128, 2 * 8 * 512], BF16)
        dsc = sb("dsc", [128, 16], F32)
        wd3 = wd[:, :].rearrange("p (i h) -> p i h", i=NI)
        ob3 = ob[:, :].rearrange("p (b h t) -> p b h t", b=2, h=8)
        psO = bs.enter_context(nc.psum_tensor(f"{P['tag']}_psO", [128, 2048], F32))
        psO2 = bs.enter_context(nc.psum_tensor(f"{P['tag']}_psO2", [128, 2048], F32))

        s_w = ph.dsem("w")
        s_wc = ph.sem("wc")
        s_hl = [ph.dsem(f"hl{i}") for i in range(4)]
        hl_cnt = [0] * 4
        s_osp = [ph.dsem("osp0"), ph.dsem("osp1")]
        osp_cnt = [0, 0]
        s_ic = ph.sem("ic")
        s_ev = ph.sem("ev")
        s_sp = ph.sem("sp")

        s_cl = ph.dsem("cl")
        def fn(e):
            return e.dma_start(out=dsc[:, :], in_=P["ds_ap"])
        ph.do("sync", fn, inc=s_cl, n=16)
        ds_t = s_cl.n

        cnt = dict(wc=0, hl=0, ic=0, ev=0, sp=0, grp=0)

        for half in range(2):
            guards = [(s_ic, cnt["ic"])] if half > 0 else []
            if u8:
                for piece in range(NI // 8):
                    g2 = guards if piece == 0 else []
                    if cnt["wc"] >= 1:
                        g2 = g2 + [(s_wc, cnt["wc"])]
                    def fn(e, piece=piece, half=half):
                        return e.dma_start(
                            out=stw[:, :].rearrange("p (i h) -> p i h", i=8),
                            in_=P["wd_ap"].rearrange("(i p) h -> p i h", p=128)[
                                :, piece * 8:(piece + 1) * 8,
                                half * 1024:(half + 1) * 1024])
                    ph.do("sync", fn, waits=g2, inc=s_w, n=16)
                    def fn(e, piece=piece):
                        return e.activation(wd3[:, piece * 8:(piece + 1) * 8, :],
                                            stw[:, :].rearrange("p (i h) -> p i h", i=8),
                                            AF.Copy, bias=-128.0)
                    ph.do("scalar", fn, waits=[(s_w, s_w.n)], inc=s_wc)
                    cnt["wc"] += 1
                wsem, wtgt = s_wc, cnt["wc"]
            else:
                npiece = (NI + 3) // 4
                for piece in range(npiece):
                    i0, i1 = piece * 4, min(NI, piece * 4 + 4)
                    def fn(e, i0=i0, i1=i1, half=half):
                        return e.dma_start(
                            out=wd3[:, i0:i1, :],
                            in_=P["wd_ap"].rearrange("(i p) h -> p i h", p=128)[
                                :, i0:i1, half * 1024:(half + 1) * 1024])
                    ph.do("gpsimd", fn, waits=guards if piece == 0 else [], inc=s_w, n=16)
                wsem, wtgt = s_w, s_w.n

            for tt8 in range(NT8):
                for ic in range(NI):
                    hslot = cnt["hl"] % 4
                    waits = []
                    if cnt["hl"] >= 4:
                        waits.append((s_ic, cnt["hl"] - 3))
                    def fn(e, ic=ic, tt8=tt8, hslot=hslot):
                        return e.dma_start(
                            out=hs[:, hslot * 512:(hslot + 1) * 512],
                            in_=P["hdn_s"][ic * 128:(ic + 1) * 128,
                                           tt8 * 512:(tt8 + 1) * 512])
                    ph.do("sync", fn, waits=waits, inc=s_hl[hslot], n=16)
                    cnt["hl"] += 1
                    hl_cnt[hslot] += 1
                    waits = [(s_hl[hslot], 16 * hl_cnt[hslot]), (wsem, wtgt)]
                    if ic == 0:
                        waits.append((s_ev, cnt["ev"]))
                    def fn(e, ic=ic, hslot=hslot, NI=NI):
                        r = None
                        for hc in range(4):
                            r = e.matmul(psO[:, hc * 512:(hc + 1) * 512],
                                         wd3[:, ic, hc * 128:(hc + 1) * 128],
                                         hs[:, hslot * 512:(hslot + 1) * 512],
                                         start=(ic == 0), stop=(ic == NI - 1))
                        for hc in range(4):
                            r = e.matmul(psO2[:, hc * 512:(hc + 1) * 512],
                                         wd3[:, ic, (hc + 4) * 128:(hc + 5) * 128],
                                         hs[:, hslot * 512:(hslot + 1) * 512],
                                         start=(ic == 0), stop=(ic == NI - 1))
                        return r
                    ph.do("tensor", fn, waits=waits, inc=s_ic)
                    cnt["ic"] += 1

                oslot = cnt["grp"] % 2
                cnt["grp"] += 1
                sp_guard = [(s_osp[oslot], 16 * osp_cnt[oslot])] if cnt["sp"] >= 2 else []
                for hc in range(4):
                    def fn(e, oslot=oslot, half=half, hc=hc):
                        return e.activation(ob3[:, oslot, hc, :],
                                            psO[:, hc * 512:(hc + 1) * 512], AF.Identity,
                                            scale=dsc[:, half * 8 + hc:half * 8 + hc + 1])
                    w0 = [(s_ic, cnt["ic"]), (s_cl, ds_t)] + sp_guard if hc == 0 else []
                    ph.do("scalar", fn, waits=w0, inc=s_ev)
                    cnt["ev"] += 1
                for hc in range(4):
                    def fn(e, oslot=oslot, half=half, hc=hc):
                        return e.tensor_scalar(
                            ob3[:, oslot, 4 + hc, :], psO2[:, hc * 512:(hc + 1) * 512],
                            dsc[:, half * 8 + 4 + hc:half * 8 + 5 + hc], None, ALU.mult)
                    w0 = [(s_ic, cnt["ic"]), (s_cl, ds_t)] + sp_guard if hc == 0 else []
                    ph.do("vector", fn, waits=w0, inc=s_ev)
                    cnt["ev"] += 1
                def fn(e, oslot=oslot, half=half, tt8=tt8):
                    return e.dma_start(
                        out=P["oT_s"].rearrange("(h p) t -> p h t", p=128)[
                            :, half * 8:(half + 1) * 8, tt8 * 512:(tt8 + 1) * 512],
                        in_=ob3[:, oslot, :, :])
                ph.do("gpsimd", fn, waits=[(s_ev, cnt["ev"])], inc=s_osp[oslot], n=16)
                cnt["sp"] += 1
                osp_cnt[oslot] += 1

        ph.do("gpsimd", lambda e: e.nop(),
              waits=[(s_osp[i], 16 * osp_cnt[i]) for i in range(2)])
        ph.do("sync", lambda e: e.nop(),
              waits=[(s_hl[i], 16 * hl_cnt[i]) for i in range(4)])
        ph.run()


def phase_ffnT(nc, stack, G, P):
    with ExitStack() as bs:
        ph = Ph(nc, bs, P["tag"], pstack=stack)
        def sb(name, shape, dtype):
            return bs.enter_context(nc.sbuf_tensor(f"{P['tag']}_{name}", shape, dtype))
        ot = sb("ot", [128, 2 * ND * 512], BF16)
        xres = sb("xres", [128, 2 * 2048], F32)
        anat = sb("anat", [128, 2 * 2048], F32)
        tnorm = sb("tnorm", [128, 2 * 2048], F32)
        mv24 = sb("mv24", [128, 24], F32)
        mvc = sb("mvc", [128, 18], F32)
        ot4 = ot[:, :].rearrange("p (b d t) -> p b d t", b=2, d=ND)
        psT = bs.enter_context(nc.psum_tensor(f"{P['tag']}_psT", [128, 2048], F32))
        psT2 = bs.enter_context(nc.psum_tensor(f"{P['tag']}_psT2", [128, 2048], F32))
        ident = G["ident"]

        s_ol = [ph.dsem("ol0"), ph.dsem("ol1")]
        ol_cnt = [0, 0]
        s_xr = [ph.dsem("xr0"), ph.dsem("xr1")]
        xr_slot_cnt = [0, 0]
        s_rol = [ph.dsem("rol0"), ph.dsem("rol1")]
        ro_slot_cnt = [0, 0]
        s_tr = ph.sem("tr")
        s_ae = ph.sem("ae")
        s_mv = ph.sem("mv")
        s_sd = ph.sem("sd")
        s_ln = ph.sem("ln")
        s_t = ph.sem("t")
        s_x = ph.sem("x")
        s_ra = ph.sem("ra")
        s_ro = ph.sem("ro")

        cnt = dict(ol=0, tr=0, ae=0, t=0, x=0, ra=0, ro=0)
        for tg in range(T // 512):
            oslot = tg % 2
            waits = [(s_tr, 4 * (tg - 1))] if tg >= 2 else []
            for piece in range(2):
                def fn(e, oslot=oslot, tg=tg, piece=piece):
                    return e.dma_start(
                        out=ot4[:, oslot, piece * 8:(piece + 1) * 8, :],
                        in_=P["oT_s"].rearrange("(d p) t -> p d t", p=128)[
                            :, piece * 8:(piece + 1) * 8, tg * 512:(tg + 1) * 512])
                ph.do("sync", fn, waits=waits if piece == 0 else [], inc=s_ol[oslot], n=16)
            cnt["ol"] += 1
            ol_cnt[oslot] += 2

            for tc in range(4):
                gi = 4 * tg + tc
                pst = psT if gi % 2 == 0 else psT2
                r0 = gi * 128
                xs = gi % 2
                waits = []
                if gi >= 2:
                    waits.append((s_ra, gi - 1))
                def fn(e, xs=xs, r0=r0):
                    return e.dma_start(out=xres[:, xs * 2048:(xs + 1) * 2048],
                                       in_=P["R_in"][r0:r0 + 128, :])
                ph.do("sync", fn, waits=waits, inc=s_xr[xs], n=16)
                cnt["x"] += 1
                xr_slot_cnt[xs] += 1

                waits = [(s_ol[oslot], 16 * ol_cnt[oslot])]
                if gi >= 2:
                    waits.append((s_ae, gi - 1))
                def fn(e, oslot=oslot, tc=tc, pst=pst):
                    r = None
                    for d in range(ND):
                        r = e.matmul(pst[:, d * 128:(d + 1) * 128],
                                     ot4[:, oslot, d, tc * 128:(tc + 1) * 128],
                                     ident[:, :], start=True, stop=True)
                    return r
                ph.do("tensor", fn, waits=waits, inc=s_tr)
                cnt["tr"] += 1
                asl = gi % 2
                waits = [(s_tr, cnt["tr"])]
                if gi >= 2:
                    waits.append((s_t, cnt["t"] - 1))
                def fn(e, pst=pst, asl=asl):
                    return e.tensor_copy(anat[:, asl * 2048:(asl + 1) * 2048],
                                         pst[:, 0:2048])
                ph.do("vector", fn, waits=waits, inc=s_ae)
                cnt["ae"] += 1
                tsl = gi % 2
                ow = [(s_rol[tsl], 16 * ro_slot_cnt[tsl])] if cnt["t"] >= 2 else []
                emit_ln(ph, anat[:, asl * 2048:(asl + 1) * 2048],
                        tnorm[:, tsl * 2048:(tsl + 1) * 2048], mv24, mvc,
                        [(s_ae, cnt["ae"])], s_mv, s_sd, s_ln, s_t, out_waits=ow)
                cnt["t"] += 1
                def fn(e, xs=xs, tsl=tsl):
                    return e.tensor_add(tnorm[:, tsl * 2048:(tsl + 1) * 2048],
                                        tnorm[:, tsl * 2048:(tsl + 1) * 2048],
                                        xres[:, xs * 2048:(xs + 1) * 2048])
                ph.do("vector", fn,
                      waits=[(s_t, cnt["t"]), (s_xr[xs], 16 * xr_slot_cnt[xs])],
                      inc=s_ra)
                cnt["ra"] += 1
                def fn(e, r0=r0, tsl=tsl):
                    return e.dma_start(out=P["R_out"][r0:r0 + 128, :],
                                       in_=tnorm[:, tsl * 2048:(tsl + 1) * 2048])
                ph.do("gpsimd", fn, waits=[(s_ra, cnt["ra"])], inc=s_rol[tsl], n=16)
                cnt["ro"] += 1
                ro_slot_cnt[tsl] += 1

        ph.do("gpsimd", lambda e: e.nop(),
              waits=[(s_rol[i], 16 * ro_slot_cnt[i]) for i in range(2)])
        ph.do("sync", lambda e: e.nop(),
              waits=[(s_xr[i], 16 * xr_slot_cnt[i]) for i in range(2)])
        ph.run()


PHASELIST = ["qkv0", "attn0", "ffnA0", "ffnB0", "ffnT0",
             "qkv1", "attn1", "ffnA1", "ffnB1", "ffnT1"]
R_BEFORE = {"qkv0": "x", "attn0": "x", "ffnA0": "R1", "ffnB0": "R1", "ffnT0": "R1",
            "qkv1": "R2", "attn1": "R2", "ffnA1": "R3", "ffnB1": "R3", "ffnT1": "R3"}
R_PRODUCED = {"attn0": "R1", "ffnT0": "R2", "attn1": "R3", "ffnT1": "y"}
SCRATCH_PROD = {"qkv0": ["qT_s", "kT_s", "vT_s", "bb_d"],
                "qkv1": ["qT_s", "kT_s", "vT_s", "bb_d"],
                "ffnA0": ["hdn_s"], "ffnA1": ["hdn_s"],
                "ffnB0": ["oT_s"], "ffnB1": ["oT_s"]}
SCRATCH_CONS = {"attn0": ["qT_s", "kT_s", "vT_s", "bb_d"],
                "attn1": ["qT_s", "kT_s", "vT_s", "bb_d"],
                "ffnB0": ["hdn_s"], "ffnB1": ["hdn_s"],
                "ffnT0": ["oT_s"], "ffnT1": ["oT_s"]}
SCRATCH_SHAPES = {"qT_s": ([D, T], BF16), "kT_s": ([D, T], BF16),
                  "vT_s": ([D, T], BF16), "bb_d": ([128, 16], F32),
                  "hdn_s": ([I1, T], BF16), "oT_s": ([D, T], BF16)}


def build_nc(parts):
    nc = bass.Bass("TRN2", target_bir_lowering=False)
    dt_in = {}

    def din(name, shape, dtype):
        if name not in dt_in:
            dt_in[name] = nc.dram_tensor(name, shape, dtype, kind="ExternalInput").ap()
        return dt_in[name]

    rnames = set()
    for p in parts:
        rnames.add(R_BEFORE[p])
        if p in R_PRODUCED:
            rnames.add(R_PRODUCED[p])
    produced_r = {R_PRODUCED[p] for p in parts if p in R_PRODUCED}
    rt = {}
    for rn in rnames:
        if rn == "y":
            rt[rn] = nc.dram_tensor("y", [T, D], F32, kind="ExternalOutput").ap()
        elif rn == "x":
            rt[rn] = din("x", [T, D], F32)
        elif rn in produced_r:
            last = parts[-1]
            consumed_later = any(R_BEFORE[q] == rn for q in PHASELIST
                                 if q not in parts)
            kind = "ExternalOutput" if consumed_later else "Internal"
            rt[rn] = nc.dram_tensor(rn, [T, D], F32, kind=kind).ap()
        else:
            rt[rn] = din(rn, [T, D], F32)

    sc = {}
    need = set()
    for p in parts:
        for s in SCRATCH_PROD.get(p, []) + SCRATCH_CONS.get(p, []):
            need.add(s)
    for s in need:
        shape, dtype = SCRATCH_SHAPES[s]
        prod_in = any(s in SCRATCH_PROD.get(p, []) for p in parts)
        cons_in = any(s in SCRATCH_CONS.get(p, []) for p in parts)
        if prod_in and cons_in:
            kind = "Internal"
        elif prod_in:
            kind = "ExternalOutput"
        else:
            kind = "ExternalInput"
        if kind == "ExternalInput":
            sc[s] = din(s, shape, dtype)
        else:
            sc[s] = nc.dram_tensor(s, shape, dtype, kind=kind).ap()

    c_id = din("c_ident", [128, 128], BF16)
    c_or = din("c_onesrow", [1, 128], BF16)
    c_oc = din("c_onescol", [128, 1], BF16)
    c_mk = din("c_masks", [512, 512], BF16)
    c_cos = din("c_cos", [128, T], BF16)
    c_sin = din("c_sin", [128, T], BF16)

    w = {
        "l0_wq": din("l0_wq", [D, 2048], F32), "l0_wk": din("l0_wk", [D, 512], F32),
        "l0_wv": din("l0_wv", [D, 512], F32), "l0_wo": din("l0_wo", [D, 2048], F32),
        "l0_bq": din("l0_bq", [128, 16], F32), "l0_bk": din("l0_bk", [128, 4], F32),
        "l0_bv": din("l0_bv", [128, 4], F32), "l0_bo": din("l0_bo", [2048], F32),
        "l0_wg": din("l0_wg", [I0 // 512, ND, 128, 512], F32),
        "l0_wu": din("l0_wu", [I0 // 512, ND, 128, 512], F32),
        "l0_wd": din("l0_wd", [I0, D], F32),
        "l0_gs": din("l0_gs", [128, I0 // 128], F32),
        "l0_us": din("l0_us", [128, I0 // 128], F32),
        "l0_ds": din("l0_ds", [128, 16], F32),
        "l1_wp": din("l1_wp", [D, 6144], F32), "l1_wo": din("l1_wo", [D, 2048], F32),
        "l1_wg": din("l1_wg", [I1 // 512, ND, 128, 512], U8),
        "l1_wu": din("l1_wu", [I1 // 512, ND, 128, 512], U8),
        "l1_wd": din("l1_wd", [I1, D], U8),
        "l1_gs": din("l1_gs", [128, I1 // 128], F32),
        "l1_us": din("l1_us", [128, I1 // 128], F32),
        "l1_ds": din("l1_ds", [128, 16], F32),
    }

    with ExitStack() as stack:
        G = {}
        G["ident"] = stack.enter_context(nc.sbuf_tensor("g_id", [128, 128], BF16))
        G["ones_row"] = stack.enter_context(nc.sbuf_tensor("g_or", [1, 128], BF16))
        G["ones_col"] = stack.enter_context(nc.sbuf_tensor("g_oc", [128, 1], BF16))
        G["masks"] = stack.enter_context(nc.sbuf_tensor("g_mk", [128, 4 * 512], BF16))
        G["cos"] = stack.enter_context(nc.sbuf_tensor("g_cos", [128, T], BF16))
        G["sin"] = stack.enter_context(nc.sbuf_tensor("g_sin", [128, T], BF16))
        ph = Ph(nc, stack, "init")
        s_c = ph.sem("c")
        for dst, src in ((G["ident"], c_id), (G["ones_row"], c_or),
                         (G["ones_col"], c_oc), (G["cos"], c_cos), (G["sin"], c_sin)):
            def fn(e, dst=dst, src=src):
                return e.dma_start(out=dst[:, :], in_=src)
            ph.do("sync", fn, inc=s_c, n=16)
        def fn(e):
            return e.dma_start(out=G["masks"][:, :].rearrange("p (j t) -> p j t", j=4),
                               in_=c_mk.rearrange("(j p) t -> p j t", p=128))
        ph.do("sync", fn, inc=s_c, n=16)
        ph.do("sync", lambda e: e.nop(), waits=[(s_c, s_c.n)])
        ph.run()

        for p in parts:
            if p == "qkv0":
                phase_qkv(nc, stack, G, dict(
                    tag=p, rope=True, nkv=4, TTQ=1024,
                    wpasses=[[("q", 16, w["l0_wq"], 0)],
                             [("k", 4, w["l0_wk"], 0), ("v", 4, w["l0_wv"], 0)]],
                    R_in=rt["x"], bq=w["l0_bq"], bk=w["l0_bk"],
                    dst={"q": (sc["qT_s"], 0), "k": (sc["kT_s"], 0),
                         "v": (sc["vT_s"], 0)},
                    bbias_d=sc["bb_d"]))
            elif p == "qkv1":
                phase_qkv(nc, stack, G, dict(
                    tag=p, rope=False, nkv=16, TTQ=2048,
                    wpasses=[[("q", 16, w["l1_wp"], 0)],
                             [("k", 16, w["l1_wp"], 16)],
                             [("v", 16, w["l1_wp"], 32)]],
                    R_in=rt["R2"],
                    dst={"q": (sc["qT_s"], 0), "k": (sc["kT_s"], 0),
                         "v": (sc["vT_s"], 0)},
                    bbias_d=sc["bb_d"]))
            elif p == "attn0":
                dbg = None
                dbgn = None
                if DEBUG_OT:
                    dbg = nc.dram_tensor("dbg_oT", [D, T], BF16,
                                         kind="ExternalOutput").ap()
                    dbgn = nc.dram_tensor("dbg_num", [1, 512], F32,
                                          kind="ExternalOutput").ap()
                phase_attn(nc, stack, G, dict(
                    tag=p, rope=True, nkv=4, wo_ap=w["l0_wo"],
                    bo=w["l0_bo"], bv=w["l0_bv"], dbg_oT=dbg, dbg_num=dbgn,
                    qT_s=sc["qT_s"], kT_s=sc["kT_s"], vT_s=sc["vT_s"],
                    bbias_d=sc["bb_d"], R_in=rt["x"], R_out=rt["R1"]))
            elif p == "attn1":
                phase_attn(nc, stack, G, dict(
                    tag=p, rope=False, nkv=16, wo_ap=w["l1_wo"],
                    qT_s=sc["qT_s"], kT_s=sc["kT_s"], vT_s=sc["vT_s"],
                    bbias_d=sc["bb_d"], R_in=rt["R2"], R_out=rt["R3"]))
            elif p == "ffnA0":
                phase_ffnA(nc, stack, G, dict(
                    tag=p, I=I0, TTF=2048, u8=False,
                    wg_blk=w["l0_wg"], wu_blk=w["l0_wu"],
                    gs_ap=w["l0_gs"], us_ap=w["l0_us"],
                    R_in=rt["R1"], hdn_s=sc["hdn_s"]))
            elif p == "ffnA1":
                phase_ffnA(nc, stack, G, dict(
                    tag=p, I=I1, TTF=1024, u8=True,
                    wg_blk=w["l1_wg"], wu_blk=w["l1_wu"],
                    gs_ap=w["l1_gs"], us_ap=w["l1_us"],
                    R_in=rt["R3"], hdn_s=sc["hdn_s"]))
            elif p == "ffnB0":
                phase_ffnB(nc, stack, G, dict(
                    tag=p, I=I0, u8=False, wd_ap=w["l0_wd"], ds_ap=w["l0_ds"],
                    hdn_s=sc["hdn_s"], oT_s=sc["oT_s"]))
            elif p == "ffnB1":
                phase_ffnB(nc, stack, G, dict(
                    tag=p, I=I1, u8=True, wd_ap=w["l1_wd"], ds_ap=w["l1_ds"],
                    hdn_s=sc["hdn_s"], oT_s=sc["oT_s"]))
            elif p == "ffnT0":
                phase_ffnT(nc, stack, G, dict(
                    tag=p, oT_s=sc["oT_s"], R_in=rt["R1"], R_out=rt["R2"]))
            elif p == "ffnT1":
                phase_ffnT(nc, stack, G, dict(
                    tag=p, oT_s=sc["oT_s"], R_in=rt["R3"], R_out=rt["y"]))
    return nc


def _bf(x):
    return np.ascontiguousarray(x.astype(ml_dtypes.bfloat16))


def _host_consts():
    ident = np.eye(128, dtype=ml_dtypes.bfloat16)
    ones_row = np.ones((1, 128), dtype=ml_dtypes.bfloat16)
    ones_col = np.ones((128, 1), dtype=ml_dtypes.bfloat16)
    j = np.arange(4)[:, None, None]
    p = np.arange(128)[None, :, None]
    t = np.arange(512)[None, None, :]
    masks = (j * 128 + p <= t).astype(ml_dtypes.bfloat16).reshape(512, 512)
    d = HD
    i = np.arange(1, d // 2 + 1, dtype=np.float64)
    thetas = 10000.0 ** (-2.0 * (i - 1.0) / d)
    ang = np.arange(1, T + 1, dtype=np.float64)[:, None] * thetas
    cos = np.concatenate([np.cos(ang), np.cos(ang)], axis=1).T
    sin = np.concatenate([np.sin(ang), np.sin(ang)], axis=1).T
    sgn = np.ones((128, 1))
    sgn[:64] = -1.0
    return {"c_ident": ident, "c_onesrow": ones_row, "c_onescol": ones_col,
            "c_masks": np.ascontiguousarray(masks),
            "c_cos": _bf(cos), "c_sin": _bf(sin * sgn)}


def _pcol(v):
    return np.ascontiguousarray(np.asarray(v).reshape(-1, 128).T)


def _blkify(W, dtype):
    I = W.shape[0]
    a = W.reshape(I // 512, 512, ND, 128).transpose(0, 2, 3, 1)
    return np.ascontiguousarray(a).astype(dtype)


def _prep_weights(inputs):
    w = {}
    w["l0_wq"] = np.ascontiguousarray(inputs["L0_attn_Wq"].T)
    w["l0_wk"] = np.ascontiguousarray(inputs["L0_attn_Wk"].T)
    w["l0_wv"] = np.ascontiguousarray(inputs["L0_attn_Wv"].T)
    w["l0_wo"] = np.ascontiguousarray(inputs["L0_attn_Wo"].T)
    w["l0_bq"] = _pcol(inputs["L0_attn_bq"])
    w["l0_bk"] = _pcol(inputs["L0_attn_bk"])
    w["l0_bv"] = _pcol(inputs["L0_attn_bv"])
    w["l0_bo"] = inputs["L0_attn_bo"]
    w["l0_wg"] = _blkify(inputs["L0_ffn_gate_w"], np.float32)
    w["l0_wu"] = _blkify(inputs["L0_ffn_up_w"], np.float32)
    w["l0_wd"] = np.ascontiguousarray(inputs["L0_ffn_down_w"].T)
    w["l0_gs"] = _pcol(inputs["L0_ffn_gate_s"])
    w["l0_us"] = _pcol(inputs["L0_ffn_up_s"])
    w["l0_ds"] = _pcol(inputs["L0_ffn_down_s"])
    w["l1_wp"] = np.ascontiguousarray(inputs["L1_attn_w_project"].T)
    w["l1_wo"] = np.ascontiguousarray(inputs["L1_attn_w_out"].T)
    w["l1_wg"] = _blkify(inputs["L1_ffn_gate_w"].astype(np.uint8), np.uint8)
    w["l1_wu"] = _blkify(inputs["L1_ffn_up_w"].astype(np.uint8), np.uint8)
    w["l1_wd"] = np.ascontiguousarray(inputs["L1_ffn_down_w"].T.astype(np.uint8))
    w["l1_gs"] = _pcol(inputs["L1_ffn_gate_scale"])
    w["l1_us"] = _pcol(inputs["L1_ffn_up_scale"])
    w["l1_ds"] = _pcol(inputs["L1_ffn_down_scale"])
    return w


DEBUG_OT = False
LAUNCH_PLAN = [PHASELIST]


def kernel(**inputs):
    x = inputs["L0_attn_x"]
    w = _prep_weights(inputs)
    consts = _host_consts()
    base = dict(w)
    base.update(consts)

    shards = [np.ascontiguousarray(x[c]) for c in range(8)]
    carry = [dict() for _ in range(8)]
    out = np.empty((8, T, D), dtype=np.float32)
    for parts in LAUNCH_PLAN:
        nc = build_nc(parts)
        need = _declared_inputs(nc)
        in_maps = []
        for c in range(8):
            im = dict(base)
            im["x"] = shards[c]
            im.update(carry[c])
            im = {k: v for k, v in im.items() if k in need}
            in_maps.append(im)
        res = run_bass_kernel_spmd(nc, in_maps, core_ids=list(range(8)))
        for c in range(8):
            for k, v in res.results[c].items():
                if k == "y":
                    out[c] = v
                else:
                    carry[c][k] = v
    return out


def _declared_inputs(nc):
    names = set()
    for alloc in nc.m.functions[0].allocations:
        if isinstance(alloc, mybir.MemoryLocationSet) and alloc.kind == "ExternalInput":
            names.add(alloc.memorylocations[0].name)
    return names
